# version 12
# baseline (speedup 1.0000x reference)
import contextlib
import numpy as np
import ml_dtypes
import concourse.bass as bass
import concourse.mybir as mybir
from concourse.bass_utils import run_bass_kernel_spmd

F32 = mybir.dt.float32
BF16 = mybir.dt.bfloat16
I32 = mybir.dt.int32
AF = mybir.ActivationFunctionType
ALU = mybir.AluOpType
AX = mybir.AxisListType.X

NCORES = 8
S = 2048
D = 1024
NQ = 512
NT = S // NQ
DT = D // 128


ENGINES = ("pe", "act", "dve", "pool", "sp")


class Prog:

    def __init__(self, nc):
        self.nc = nc
        self.streams = {e: [] for e in ENGINES}
        self.counts = {s: 0 for s in ENGINES}
        self.semnames = list(ENGINES)
        self.lw = {}
        self.readers = {}
        self.wm = {e: {} for e in ENGINES}

    def _sem(self, name):
        if name not in self.counts:
            self.counts[name] = 0
            self.semnames.append(name)
        return name

    def _waits(self, eng, reads, writes):
        need = {}

        def add(s, v):
            if s.startswith("D_"):
                v = self.counts[s]
            if need.get(s, 0) < v:
                need[s] = v

        for t in reads:
            if t in self.lw:
                add(*self.lw[t])
        for t in writes:
            rd = self.readers.get(t)
            if rd:
                for s, v in rd.items():
                    add(s, v)
            elif t in self.lw:
                add(*self.lw[t])
        w = self.wm[eng]
        out = []
        for s, v in need.items():
            if w.get(s, 0) < v:
                out.append((s, v))
                w[s] = v
        return out

    def _book(self, sem, val, reads, writes):
        for t in writes:
            self.lw[t] = (sem, val)
            self.readers.pop(t, None)
        for t in reads:
            d = self.readers.setdefault(t, {})
            if d.get(sem, 0) < val:
                d[sem] = val

    def op(self, eng, fn, reads=(), writes=()):
        waits = self._waits(eng, reads, writes)
        self.counts[eng] += 1
        val = self.counts[eng]
        self.streams[eng].append((fn, waits, eng, val, 1))
        self._book(eng, val, reads, writes)

    def dma(self, eng, out, in_, reads=(), writes=(), slot=None, **kw):
        assert writes, "dma needs a write token"
        sem = self._sem("D_" + (slot or writes[0]))
        waits = self._waits(eng, reads, writes)
        self.counts[sem] += 16
        val = self.counts[sem]

        def fn(e, out=out, in_=in_, kw=kw):
            return e.dma_start(out=out, in_=in_, **kw)

        self.streams[eng].append((fn, waits, sem, val, 16))
        self._book(sem, val, reads, writes)

    def barrier(self, engines=ENGINES):
        for eng in engines:
            w = self.wm[eng]
            waits = []
            for s in self.semnames:
                c = self.counts[s]
                if c > 0 and w.get(s, 0) < c:
                    waits.append((s, c))
                    w[s] = c
            if waits:
                self.streams[eng].append((None, waits, None, 0, 0))

    def emit(self, est):
        semh = {s: est.enter_context(self.nc.semaphore(s)) for s in self.semnames
                if self.counts[s] > 0 or s in ENGINES}
        block = est.enter_context(self.nc.Block())

        def run(eng_name):
            def body(e):
                for fn, waits, sem, val, inc in self.streams[eng_name]:
                    for (ws, wv) in waits:
                        e.wait_ge(semh[ws], wv)
                    if fn is not None:
                        inst = fn(e)
                        if sem is not None:
                            inst.then_inc(semh[sem], inc)
            return body

        block.tensor(run("pe"))
        block.scalar(run("act"))
        block.vector(run("dve"))
        block.gpsimd(run("pool"))
        block.sync(run("sp"))



def f32(a):
    return np.ascontiguousarray(np.asarray(a), dtype=np.float32)


def to_bf16(a):
    return np.ascontiguousarray(np.asarray(a, dtype=np.float32).astype(ml_dtypes.bfloat16))


SLOPES = 2.0 ** (-0.5 * np.arange(1, 17))
ALIBI_SLACK = 220.0
DCUT = [int(np.ceil(ALIBI_SLACK / s)) for s in SLOPES]
ALIBI_MARGIN = 4.0
PREMASK_HEADS = [h for h in range(16) if SLOPES[h] * 511 - ALIBI_MARGIN > 78.0]


def build_attn_consts():
    c = {}
    c["ones128c"] = to_bf16(np.ones((128, 1)))
    c["ones1r"] = f32(np.ones((1, 128)))
    bd = np.zeros((128, 128), np.float32)
    bd[:64, :64] = 1.0
    bd[64:, 64:] = 1.0
    c["bd64"] = to_bf16(bd)
    e2 = np.zeros((2, 128), np.float32)
    e2[0, :64] = 1.0
    e2[1, 64:] = 1.0
    c["e2pair"] = e2
    m = np.zeros((128, 4 * 512), np.float32)
    mneg = np.zeros((128, 4 * 512), np.float32)
    for j in range(4):
        kk = 128 * j + np.arange(128)[:, None]
        qq = np.arange(512)[None, :]
        alive = (kk <= qq)
        m[:, j * 512:(j + 1) * 512] = alive
        mneg[:, j * 512:(j + 1) * 512] = np.where(alive, 0.0, -1e9)
    c["diagmask"] = to_bf16(m)
    c["diagneg"] = to_bf16(mneg)
    slb = np.zeros((128, NT * 16 * 16), np.float32)
    p = np.arange(128)[:, None]
    for qc in range(NT):
        for h in range(16):
            for t in range(16):
                col = (qc * 16 + h) * 16 + t
                slb[:, col:col + 1] = SLOPES[h] * (128 * t + p - (512 * qc + 256))
    c["slb"] = f32(slb)
    v = -8.0 * SLOPES[:, None] * (np.arange(512)[None, :] - 256.0)
    hi = v.astype(ml_dtypes.bfloat16).astype(np.float64)
    lo = (v - hi).astype(ml_dtypes.bfloat16).astype(np.float64)
    vhl = np.zeros((2, 16 * 512), np.float32)
    vhl[0] = hi.reshape(-1)
    vhl[1] = lo.reshape(-1)
    c["vhl"] = to_bf16(vhl)
    return c



def emit_big_attn(P, mem, W, opts):
    pre = opts["pre"]
    alibi = opts["alibi"]
    xres_in = opts["xres_in"]
    xres_out = opts["xres_out"]
    dram = opts["dram"]

    xbf = mem["xbf"]
    kt = mem["kt"]
    vaug = mem["vaug"]
    qt = mem["qt"]
    kaug = mem["kaug"]
    qaug = mem["qaug"]
    ptile = mem["ptile"]
    attO = mem["attO"]
    xc32 = mem["xc32"]
    sq = mem["sq"]
    xcbf = mem["xcbf"]
    sm = mem["smalls"]
    ssum = mem["ssum"]
    r16 = mem["r16"]
    r2 = mem["r2"]
    brep = mem["brep"]
    biasm = mem["biasm"]

    rrep = mem["rrep"]
    mrrep = mem["mrrep"]
    ps = mem["psum"]
    C = W["consts"]

    wA, wB = W["setA"], W["setB"]
    bq, bk, bv, bo = W.get("bq"), W.get("bk"), W.get("bv"), W.get("bo")
    for dt in range(DT):
        P.dma("pool", wA[dt][:, :], W["wk_d"][dt * 128:(dt + 1) * 128, :], writes=["wsetA"])
        P.dma("pool", wB[dt][:, :], W["wv_d"][dt * 128:(dt + 1) * 128, :], writes=["wsetB"])
    wk = wA
    wv = wB

    def tok(n):
        return pre + n

    VW = 16 * 65

    def mseps(e):
        return e.memset(mem["eps11"][0:1, 0:1], 1e-5)
    P.op("dve", mseps, writes=["eps11"])
    if alibi:
        for i in range(2):
            def msk1(e, i=i):
                return e.memset(kaug[i][64:66, :], 1.0)
            P.op("dve", msk1, writes=[tok(f"ka{i}")])

    for p8 in range(8):
        for ch in range(NT):
            def mmk(e, p8=p8, ch=ch):
                last = None
                for dt in range(DT):
                    last = e.matmul(ps["a"][:, :NQ], wk[dt][:, p8 * 128:(p8 + 1) * 128],
                                    xbf[dt][:, ch * NQ:(ch + 1) * NQ],
                                    start=(dt == 0), stop=(dt == DT - 1))
                return last
            P.op("pe", mmk, reads=["wsetA", "xbf"], writes=["ps_a"])

            def evk(e, p8=p8, ch=ch):
                if bk is not None:
                    return e.activation(kt[p8][:, ch * NQ:(ch + 1) * NQ], ps["a"][:, :NQ],
                                        AF.Identity, bias=bk[:, p8:p8 + 1])
                return e.copy(kt[p8][:, ch * NQ:(ch + 1) * NQ], ps["a"][:, :NQ])
            P.op("act", evk, reads=["ps_a"] + ([bk.name] if bk is not None else []),
                 writes=[tok(f"kt{p8}")])

    for tt in range(16):
        def msv(e, tt=tt):
            return e.memset(vaug[tt][:, :], 0.0)
        P.op("dve", msv, writes=[tok(f"va{tt}")])

        def onecol(e, tt=tt):
            ap = bass.AP(vaug[tt], 64, [[VW, 128], [65, 16], [1, 1]])
            return e.memset(ap, 1.0)
        P.op("dve", onecol, reads=[tok(f"va{tt}")], writes=[tok(f"va{tt}")])
    for tt in range(16):
        for half in range(2):
            def mmv(e, tt=tt, half=half):
                last = None
                for dt in range(DT):
                    last = e.matmul(ps["a"][:, :NQ], xbf[dt][:, tt * 128:(tt + 1) * 128],
                                    wv[dt][:, half * 512:(half + 1) * 512],
                                    start=(dt == 0), stop=(dt == DT - 1))
                return last
            P.op("pe", mmv, reads=["wsetB", "xbf"], writes=["ps_a"])

            def evv(e, tt=tt, half=half):
                o = bass.AP(vaug[tt], (half * 8) * 65, [[VW, 128], [65, 8], [1, 64]])
                i = ps["a"].ap().rearrange("p (h c) -> p h c", c=64)[:, :, :]
                return e.copy(o, i)
            P.op("act", evv, reads=["ps_a"], writes=[tok(f"va{tt}")])

    for p8 in range(8):
        for ch in range(NT):
            def sqk(e, p8=p8, ch=ch):
                return e.activation(sq[:, :NQ], kt[p8][:, ch * NQ:(ch + 1) * NQ], AF.Square)
            P.op("act", sqk, reads=[tok(f"kt{p8}")], writes=[tok("sq")])

            def mmn(e):
                return e.matmul(ps["b"][:, :NQ], C["bd64"][:, :], sq[:, :NQ], start=True, stop=True)
            P.op("pe", mmn, reads=[tok("sq"), "c"], writes=["ps_b"])

            def rmx(e, p8=p8, ch=ch):
                return e.tensor_reduce(sm[:, p8 * NT + ch:p8 * NT + ch + 1], ps["b"][:, :NQ],
                                       axis=AX, op=ALU.max)
            P.op("dve", rmx, reads=["ps_b"], writes=[tok("kn")])

    def rk2(e):
        return e.tensor_reduce(sm[:, 32:33], sm[:, 0:32], axis=AX, op=ALU.max)
    P.op("dve", rk2, reads=[tok("kn")], writes=[tok("kn1")])
    P.dma("sp", bass.AP(dram["bounce"].tensor, 0, [[1, 128], [1, 1]]), sm[:, 32:33],
          reads=[tok("kn1")], writes=[tok("dbK")], slot="dbK")
    P.dma("sp", mem["kbrow"][0:1, :], bass.AP(dram["bounce"].tensor, 0, [[128, 1], [1, 128]]),
          reads=[tok("dbK")], writes=[tok("knrow")], slot="knrow")

    def rk3(e):
        return e.tensor_reduce(sm[0:1, 33:34], mem["kbrow"][0:1, :], axis=AX, op=ALU.max)
    P.op("dve", rk3, reads=[tok("knrow")], writes=[tok("kmax")])

    dbg = opts.get("debug", {})
    if "kt0" in dbg:
        P.dma("act", dbg["kt0"], kt[0][:, :], reads=[tok("kt0")], writes=["dbg_kt0"])
    if "vaug0" in dbg:
        P.dma("act", dbg["vaug0"], vaug[0][:, :], reads=[tok("va0")], writes=["dbg_vaug0"])
    for dt in range(DT):
        P.dma("pool", wA[dt][:, :], W["wq_d"][dt * 128:(dt + 1) * 128, :], writes=["wsetA"])
        P.dma("pool", wB[dt][:, :], W["wo_d"][dt * 128:(dt + 1) * 128, :], writes=["wsetB"])
    wq = wA
    wo = wB

    def qphaseA(qc):
        qlo = qc * NQ

        for p8 in range(8):
            def mmq(e, p8=p8, qc=qc):
                last = None
                for dt in range(DT):
                    last = e.matmul(ps["a"][:, :NQ], wq[dt][:, p8 * 128:(p8 + 1) * 128],
                                    xbf[dt][:, qc * NQ:(qc + 1) * NQ],
                                    start=(dt == 0), stop=(dt == DT - 1))
                return last
            P.op("pe", mmq, reads=["wsetA", "xbf"], writes=["ps_a"])

            def evq(e, p8=p8):
                if bq is not None:
                    return e.activation(qt[p8][:, :], ps["a"][:, :NQ], AF.Identity, bias=bq[:, p8:p8 + 1])
                return e.copy(qt[p8][:, :], ps["a"][:, :NQ])
            P.op("act", evq, reads=["ps_a"] + ([bq.name] if bq is not None else []),
                 writes=[tok(f"qt{p8}")])

        for p8 in range(8):
            def sqq(e, p8=p8):
                return e.activation(sq[:, :NQ], qt[p8][:, :], AF.Square)
            P.op("act", sqq, reads=[tok(f"qt{p8}")], writes=[tok("sq")])

            def mmn2(e):
                return e.matmul(ps["b"][:, :NQ], C["bd64"][:, :], sq[:, :NQ], start=True, stop=True)
            P.op("pe", mmn2, reads=[tok("sq"), "c"], writes=["ps_b"])

            def rmq(e, p8=p8):
                return e.tensor_reduce(sm[:, 34 + p8:35 + p8], ps["b"][:, :NQ], axis=AX, op=ALU.max)
            P.op("dve", rmq, reads=["ps_b"], writes=[tok("qn")])

        def rq2(e):
            return e.tensor_reduce(sm[:, 42:43], sm[:, 34:42], axis=AX, op=ALU.max)
        P.op("dve", rq2, reads=[tok("qn")], writes=[tok("qn1")])

    def qphaseB(qc):
        qlo = qc * NQ
        P.dma("sp", bass.AP(dram["bounce"].tensor, 128, [[1, 128], [1, 1]]), sm[:, 42:43],
              reads=[tok("qn1")], writes=[tok("dbQ")], slot="dbQ")
        P.dma("sp", mem["qbrow"][0:1, :], bass.AP(dram["bounce"].tensor, 128, [[128, 1], [1, 128]]),
              reads=[tok("dbQ")], writes=[tok("qnrow")], slot="qnrow")

        def rq3(e):
            return e.tensor_reduce(sm[0:1, 43:44], mem["qbrow"][0:1, :], axis=AX, op=ALU.max)
        P.op("dve", rq3, reads=[tok("qnrow")], writes=[tok("qmax")])

        def bprod(e):
            return e.tensor_tensor(sm[0:1, 44:45], sm[0:1, 43:44], sm[0:1, 33:34], ALU.mult)
        P.op("dve", bprod, reads=[tok("qmax"), tok("kmax")], writes=[tok("bp")])

        def bln(e):
            return e.activation(sm[0:1, 44:45], sm[0:1, 44:45], AF.Ln)
        P.op("act", bln, reads=[tok("bp")], writes=[tok("bp")])
        margin = ALIBI_MARGIN if alibi else 4.0

        def bexp(e):
            return e.activation(sm[0:1, 44:45], sm[0:1, 44:45], AF.Exp, scale=0.5)
        P.op("act", bexp, reads=[tok("bp")], writes=[tok("bp")])

        def bneg(e, margin=margin):
            return e.tensor_scalar(sm[0:1, 45:46], sm[0:1, 44:45], -1.0 / 64.0, -margin,
                                   ALU.mult, ALU.add)
        P.op("dve", bneg, reads=[tok("bp")], writes=[tok("nb")])
        P.dma("sp", bass.AP(dram["bounce"].tensor, 256, [[1, 1], [1, 1]]), sm[0:1, 45:46],
              reads=[tok("nb")], writes=[tok("dbB")], slot="dbB")
        P.dma("sp", brep[:, 0:1], bass.AP(dram["bounce"].tensor, 256, [[0, 128], [1, 1]]),
              reads=[tok("dbB")], writes=[tok("brep")], slot="brep")
        if alibi:
            def mkbias(e, qc=qc):
                return e.tensor_scalar(biasm[:, :], W["slb"][:, qc * 256:(qc + 1) * 256],
                                       brep[:, 0:1], None, ALU.add)
            P.op("dve", mkbias, reads=["c", tok("brep")], writes=[tok("biasm")])

        if qc == 0 and "qt0" in opts.get("debug", {}):
            P.dma("act", opts["debug"]["qt0"], qt[0][:, :], reads=[tok("qt0")], writes=["dbg_qt0"])
            P.dma("act", opts["debug"]["brep"], brep[:, 0:1], reads=[tok("brep")], writes=["dbg_brep"])

    qphaseA(0)
    qphaseB(0)
    for qc in range(NT):
        qlo = qc * NQ
        qtile = qc * 4
        for h in range(16):
            p8 = h // 2
            base = 64 * (h % 2)
            par = h % 2
            if alibi:
                kb, qb = kaug[par], qaug[par]

                def cpk(e, p8=p8, base=base, kb=kb):
                    return e.tensor_copy(kb[0:64, :], kt[p8][base:base + 64, :])
                P.op("dve", cpk, reads=[tok(f"kt{p8}")], writes=[tok(f"ka{par}")])

                def cpq(e, p8=p8, base=base, qb=qb):
                    return e.tensor_copy(qb[0:64, :], qt[p8][base:base + 64, :])
                P.op("dve", cpq, reads=[tok(f"qt{p8}")], writes=[tok(f"qa{par}")])

                P.dma("sp", qb[64:66, :], W["vhl_d"][0:2, h * 512:(h + 1) * 512],
                      writes=[tok(f"qa{par}")], slot=f"qa{par}")
                lo_t = max(0, (qlo - DCUT[h]) // 128)
            else:
                lo_t = 0
            ktiles = list(range(lo_t, qtile + 4))
            nk = len(ktiles)
            premask = alibi and (h in PREMASK_HEADS)

            def emit_score(ki):
                t = ktiles[ki]
                pband = "c" if (ki % 2 == 0) else "d"
                if alibi:
                    def mms(e, t=t, par=par, pband=pband):
                        return e.matmul(ps[pband][:, :NQ], kaug[par][0:66, t * 128:(t + 1) * 128],
                                        qaug[par][0:66, :], start=True, stop=True)
                    P.op("pe", mms, reads=[tok(f"ka{par}"), tok(f"qa{par}")], writes=["ps_" + pband])
                else:
                    def mms(e, t=t, p8=p8, base=base, pband=pband):
                        return e.matmul(ps[pband][:, :NQ],
                                        kt[p8][base:base + 64, t * 128:(t + 1) * 128],
                                        qt[p8][base:base + 64, :], start=True, stop=True)
                    P.op("pe", mms, reads=[tok(f"kt{p8}"), tok(f"qt{p8}")], writes=["ps_" + pband])
                dj = t - qtile
                if premask and dj >= 0:
                    def pmk(e, dj=dj, pband=pband):
                        return e.scalar_tensor_tensor(ps[pband][:, :NQ], ps[pband][:, :NQ], 1.0,
                                                      C["diagneg"][:, dj * 512:(dj + 1) * 512],
                                                      ALU.mult, ALU.add)
                    P.op("dve", pmk, reads=["ps_" + pband, "c"], writes=["ps_" + pband])
                pt = ptile[ki % 2]
                if alibi:
                    def evx(e, t=t, h=h, pband=pband, pt=pt):
                        return e.activation(pt[:, :], ps[pband][:, :NQ], AF.Exp,
                                            bias=biasm[:, h * 16 + t:h * 16 + t + 1], scale=0.125)
                    P.op("act", evx, reads=["ps_" + pband, tok("biasm")], writes=[tok(f"pt{ki%2}")])
                else:
                    def evx(e, pband=pband, pt=pt):
                        return e.activation(pt[:, :], ps[pband][:, :NQ], AF.Exp,
                                            bias=brep[:, 0:1], scale=0.125)
                    P.op("act", evx, reads=["ps_" + pband, tok("brep")], writes=[tok(f"pt{ki%2}")])
                if dj >= 0 and not premask:
                    def msk(e, dj=dj, pt=pt):
                        return e.tensor_tensor(pt[:, :], pt[:, :],
                                               C["diagmask"][:, dj * 512:(dj + 1) * 512], ALU.mult)
                    P.op("dve", msk, reads=[tok(f"pt{ki%2}"), "c"], writes=[tok(f"pt{ki%2}")])

            def emit_pv(ki):
                t = ktiles[ki]
                def mpv(e, t=t, h=h, ki=ki, nk=nk):
                    return e.matmul(ps["e"][0:65, :NQ], vaug[t][:, h * 65:h * 65 + 65],
                                    ptile[ki % 2][:, :], start=(ki == 0), stop=(ki == nk - 1))
                P.op("pe", mpv, reads=[tok(f"va{t}"), tok(f"pt{ki%2}")], writes=["ps_e"])

            emit_score(0)
            for ki in range(1, nk):
                emit_score(ki)
                emit_pv(ki - 1)
            emit_pv(nk - 1)

            def evo(e, p8=p8, base=base):
                return e.copy(attO[p8][base:base + 64, :], ps["e"][0:64, :NQ])
            P.op("act", evo, reads=["ps_e"], writes=[tok(f"at{p8}")])

            sb = ssum[h % 2]
            def evs(e, sb=sb):
                return e.copy(sb[0:1, :], ps["e"][64:65, :NQ])
            P.op("act", evs, reads=["ps_e"], writes=[tok(f"ss{h%2}")])
            P.dma("sp", bass.AP(dram["sums"].tensor, h * 512, [[512, 1], [1, 512]]), sb[0:1, :],
                  reads=[tok(f"ss{h%2}")], writes=[tok("dS")], slot="dS")

        if qc + 1 < NT:
            qphaseA(qc + 1)
        dbg = opts.get("debug", {})
        if qc == 0 and "attO0" in dbg:
            P.dma("act", dbg["attO0"], attO[0][:, :], reads=[tok("at0")], writes=["dbg_attO0"])
        P.dma("sp", r16[:, :], bass.AP(dram["sums"].tensor, 0, [[512, 16], [1, 512]]),
              reads=[tok("dS")], writes=[tok("r16")], slot="r16")

        if qc == 0 and "rawsums" in opts.get("debug", {}):
            P.dma("act", opts["debug"]["rawsums"], r16[:, :], reads=[tok("r16")], writes=["dbg_rawsums"])

        def lns(e):
            return e.activation(r16[:, :], r16[:, :], AF.Ln)
        P.op("act", lns, reads=[tok("r16")], writes=[tok("r16")])

        def exns(e):
            return e.activation(r16[:, :], r16[:, :], AF.Exp, scale=-1.0)
        P.op("act", exns, reads=[tok("r16")], writes=[tok("r16")])
        P.dma("sp", dram["r16d"], r16[:, :], reads=[tok("r16")], writes=[tok("dR")], slot="dR")
        if qc == 0 and "r16" in opts.get("debug", {}):
            P.dma("act", opts["debug"]["r16"], r16[:, :], reads=[tok("r16")], writes=["dbg_r16"])
        for p8 in range(8):
            rp = r2[p8 % 2]
            P.dma("sp", rp[:, :], bass.AP(dram["r16d"].tensor, 0, [[512, 2], [1024, 8], [1, 512]])[:, p8, :],
                  reads=[tok("dR")], writes=[tok(f"r2{p8%2}")], slot=f"r2{p8%2}")
            def mrep(e, p8=p8, rp=rp):
                return e.matmul(ps["b"][:, :NQ], C["e2pair"][0:2, :], rp[0:2, :],
                                start=True, stop=True)
            P.op("pe", mrep, reads=[tok(f"r2{p8%2}"), "c"], writes=["ps_b"])
            def nrm(e, p8=p8):
                return e.tensor_tensor(attO[p8][:, :], attO[p8][:, :], ps["b"][:, :NQ], ALU.mult)
            P.op("dve", nrm, reads=["ps_b", tok(f"at{p8}")], writes=[tok(f"at{p8}")])
            if bv is not None:
                def nrm1(e, p8=p8):
                    return e.activation(attO[p8][:, :], attO[p8][:, :], AF.Identity,
                                        bias=bv[:, p8:p8 + 1])
                P.op("act", nrm1, reads=[tok(f"at{p8}"), bv.name], writes=[tok(f"at{p8}")])

        if qc + 1 < NT:
            qphaseB(qc + 1)
        for dt in range(DT):
            P.dma(("sp", "act", "pool")[dt % 3], xc32[dt][:, :],
                  xres_in[dt * 128:(dt + 1) * 128, qlo:qlo + NQ],
                  writes=[tok(f"xl{dt}")], slot=f"xl{dt}")
            def mmo(e, dt=dt):
                last = None
                for ot in range(DT):
                    last = e.matmul(ps["a"][:, :NQ], wo[ot][:, dt * 128:(dt + 1) * 128],
                                    attO[ot][:, :], start=(ot == 0), stop=(ot == DT - 1))
                return last
            P.op("pe", mmo, reads=["wsetB"] + [tok(f"at{o}") for o in range(DT)], writes=["ps_a"])

            if bo is not None:
                def radd(e, dt=dt):
                    return e.scalar_tensor_tensor(xc32[dt][:, :], ps["a"][:, :NQ],
                                                  bo[:, dt:dt + 1], xc32[dt][:, :],
                                                  ALU.add, ALU.add)
            else:
                def radd(e, dt=dt):
                    return e.tensor_tensor(xc32[dt][:, :], ps["a"][:, :NQ], xc32[dt][:, :], ALU.add)
            P.op("dve", radd, reads=[tok(f"xl{dt}"), "ps_a"] + ([bo.name] if bo is not None else []),
                 writes=[tok(f"xl{dt}")])

        emit_postln(P, mem, C, tok, xc32, xbf, qlo, xres_out, pre)


def emit_postln(P, mem, C, tok, xc32, xbf, qlo, xres_out, pre):
    sq = mem["sq"]
    xcbf = mem["xcbf"]
    statm = mem["statm"]
    stats = mem["stats"]
    statm2 = mem["rrep"]
    rrep = mem["rrep"]
    mrrep = mem["mrrep"]
    ps = mem["psum"]

    for dt in range(DT):
        def cbx(e, dt=dt):
            return e.tensor_copy(xcbf[:, :], xc32[dt][:, :])
        P.op("dve", cbx, reads=[tok(f"xl{dt}")], writes=[tok("xcbf")])

        def mmsx(e, dt=dt):
            return e.matmul(ps["f"][0:1, :NQ], C["ones128c"][:, 0:1], xcbf[:, :],
                            start=(dt == 0), stop=(dt == DT - 1))
        P.op("pe", mmsx, reads=[tok("xcbf"), "c"], writes=["ps_f"])

        def sqx(e, dt=dt):
            return e.activation(sq[:, :NQ], xc32[dt][:, :], AF.Square)
        P.op("act", sqx, reads=[tok(f"xl{dt}")], writes=[tok("sq")])

        def mmsq(e, dt=dt):
            return e.matmul(ps["g"][0:1, :NQ], C["ones128c"][:, 0:1], sq[:, :NQ],
                            start=(dt == 0), stop=(dt == DT - 1))
        P.op("pe", mmsq, reads=[tok("sq"), "c"], writes=["ps_g"])

    def st1(e):
        return e.tensor_scalar(statm[0:1, :], ps["f"][0:1, :NQ], 1.0 / D, None, ALU.mult)
    P.op("dve", st1, reads=["ps_f"], writes=[tok("stm")])

    def st2(e):
        return e.tensor_scalar(stats[0:1, :], ps["g"][0:1, :NQ], 1.0 / D, None, ALU.mult)
    P.op("dve", st2, reads=["ps_g"], writes=[tok("sts")])

    def st3(e):
        return e.tensor_tensor(statm2[0:1, :], statm[0:1, :], statm[0:1, :], ALU.mult)
    P.op("dve", st3, reads=[tok("stm")], writes=[tok("rrep")])

    def st4(e):
        return e.tensor_tensor(stats[0:1, :], stats[0:1, :], statm2[0:1, :], ALU.subtract)
    P.op("dve", st4, reads=[tok("sts"), tok("rrep")], writes=[tok("sts")])

    def st5(e):
        return e.activation(stats[0:1, :], stats[0:1, :], AF.Ln, bias=mem["eps11"][0:1, 0:1])
    P.op("act", st5, reads=[tok("sts"), "eps11"], writes=[tok("sts")])

    def st6(e):
        return e.activation(stats[0:1, :], stats[0:1, :], AF.Exp, scale=-0.5)
    P.op("act", st6, reads=[tok("sts")], writes=[tok("sts")])

    def st7(e):
        return e.tensor_tensor(statm[0:1, :], statm[0:1, :], stats[0:1, :], ALU.mult)
    P.op("dve", st7, reads=[tok("stm"), tok("sts")], writes=[tok("stm")])

    def mr1(e):
        return e.matmul(ps["f"][:, :NQ], C["ones1r"][0:1, :], stats[0:1, :], start=True, stop=True)
    P.op("pe", mr1, reads=[tok("sts"), "c"], writes=["ps_f"])

    def mr2(e):
        return e.matmul(ps["g"][:, :NQ], C["ones1r"][0:1, :], statm[0:1, :], start=True, stop=True)
    P.op("pe", mr2, reads=[tok("stm"), "c"], writes=["ps_g"])

    def evr(e):
        return e.copy(rrep[:, :], ps["f"][:, :NQ])
    P.op("act", evr, reads=["ps_f"], writes=[tok("rrep")])

    def evm(e):
        return e.copy(mrrep[:, :], ps["g"][:, :NQ])
    P.op("act", evm, reads=["ps_g"], writes=[tok("mrrep")])

    for dt in range(DT):
        def ap1(e, dt=dt):
            return e.tensor_tensor(xc32[dt][:, :], xc32[dt][:, :], rrep[:, :], ALU.mult)
        P.op("dve", ap1, reads=[tok(f"xl{dt}"), tok("rrep")], writes=[tok(f"xl{dt}")])

        def ap2(e, dt=dt):
            return e.tensor_tensor(xc32[dt][:, :], xc32[dt][:, :], mrrep[:, :], ALU.subtract)
        P.op("dve", ap2, reads=[tok(f"xl{dt}"), tok("mrrep")], writes=[tok(f"xl{dt}")])
        P.dma(("sp", "act", "pool")[dt % 3], xres_out[dt * 128:(dt + 1) * 128, qlo:qlo + NQ],
              xc32[dt][:, :], reads=[tok(f"xl{dt}")], writes=[pre + f"dxo{dt}"],
              slot=f"dxo{dt}")

        if xbf is not None:
            def cb2(e, dt=dt):
                return e.tensor_copy(xbf[dt][:, qlo:qlo + NQ], xc32[dt][:, :])
            P.op("dve", cb2, reads=[tok(f"xl{dt}")], writes=["xbf"])



def make_shared(nc, est):
    E = est.enter_context
    sh = {}
    sh["xbf"] = [E(nc.sbuf_tensor(f"xbf{i}", [128, S], BF16)) for i in range(DT)]
    sh["psum"] = {k: E(nc.psum_tensor(f"ps_{k}", [128, 512], F32)) for k in "abcdefgh"}
    sh["psumh"] = sh["psum"]["h"].bitcast(BF16)
    sh["eps11"] = E(nc.sbuf_tensor("eps11", [1, 1], F32))
    sh["eps18"] = E(nc.sbuf_tensor("eps18", [128, 1], F32))
    return sh


def alloc_attn_buffers(alloc, sh):
    mem = {}
    mem["xbf"] = sh["xbf"]
    mem["kt"] = [alloc(f"kt{i}", [128, S], BF16) for i in range(8)]
    mem["vaug"] = [alloc(f"vaug{i}", [128, 16 * 65], BF16) for i in range(16)]
    mem["qt"] = [alloc(f"qt{i}", [128, NQ], BF16) for i in range(8)]
    mem["kaug"] = [alloc(f"kaug{i}", [128, S], BF16) for i in range(2)]
    mem["qaug"] = [alloc(f"qaug{i}", [128, NQ], BF16) for i in range(2)]
    mem["ptile"] = [alloc(f"ptile{i}", [128, NQ], BF16) for i in range(2)]
    mem["attO"] = [alloc(f"attO{i}", [128, NQ], BF16) for i in range(DT)]
    mem["xc32"] = [alloc(f"xc32_{i}", [128, NQ], F32) for i in range(DT)]
    mem["sq"] = alloc("sq", [128, NQ], BF16)
    mem["xcbf"] = alloc("xcbf", [128, NQ], BF16)
    mem["smalls"] = alloc("smalls", [128, 48], F32)
    mem["r16"] = alloc("r16", [16, 512], F32)
    mem["r2"] = [alloc(f"r2_{i}", [2, 512], F32) for i in range(2)]
    mem["ssum"] = [t for t in mem["r2"]]
    mem["brep"] = alloc("brep", [128, 1], F32)
    mem["biasm"] = alloc("biasm", [128, 256], F32)
    mem["statm"] = alloc("statm", [1, NQ], F32)
    mem["stats"] = alloc("stats", [1, NQ], F32)
    mem["eps11"] = sh["eps11"]
    mem["kbrow"] = alloc("kbrow", [1, 128], F32)
    mem["qbrow"] = alloc("qbrow", [1, 128], F32)
    mem["rrep"] = alloc("rrep", [128, NQ], F32)
    mem["mrrep"] = alloc("mrrep", [128, NQ], F32)
    mem["psum"] = sh["psum"]
    return mem


def load_consts(P, nc, alloc, consts_np, which, queue="sp"):
    out = {}
    for k in which:
        a = consts_np[k]
        dt = BF16 if a.dtype == ml_dtypes.bfloat16 else F32
        h = nc.dram_tensor("c_" + k, list(a.shape), dt, kind="ExternalInput")
        t = alloc("sc_" + k, list(a.shape), dt)
        P.dma(queue, t[:, :], h.ap(), writes=["c"], slot="c")
        out[k] = t
    return out



def build_ffn_consts():
    c = {}
    DH, WIN = 64, 512
    half = np.arange(0, DH, 2, dtype=np.float64)
    inv_freq = 1.0 / 10000.0 ** (half / DH)
    t = np.arange(S, dtype=np.float64)
    freqs = t[:, None] * inv_freq[None, :]
    freqs2 = np.concatenate([freqs, freqs], 1)
    base = (half + 0.4 * DH) / (1.4 * DH)
    power = (t - S // 2) / float(WIN // 2)
    scale = base[None, :] ** power[:, None]
    scale2 = np.concatenate([scale, scale], 1)
    cos = np.cos(freqs2); sin = np.sin(freqs2)
    def two(v):
        return np.concatenate([v.T, v.T], 0)
    c["cs8q"] = to_bf16(two(8.0 * cos * scale2))
    c["sn8q"] = to_bf16(two(8.0 * sin * scale2))
    c["csk"] = to_bf16(two(cos / scale2))
    c["snk"] = to_bf16(two(sin / scale2))
    R = np.zeros((128, 128), np.float32)
    for m in range(128):
        b = (m // 64) * 64
        dm = m % 64
        if dm < 32:
            R[b + dm + 32, m] = -1.0
        else:
            R[b + dm - 32, m] = 1.0
    c["rot"] = to_bf16(R)
    m = np.zeros((128, 4 * 512), np.float32)
    for djj in range(4):
        kk = 128 * djj + np.arange(128)[:, None]
        jj = np.arange(512)[None, :]
        m[:, djj * 512:(djj + 1) * 512] = (kk >= jj)
    c["prevmask"] = to_bf16(m)
    return c



def emit_ffn_layer(P, mem, W, opts):
    pre = opts["pre"]
    xres_in = opts["xres_in"]
    xres_out = opts["xres_out"]
    dram = opts["dram"]
    C = W["consts"]
    ps = mem["psum"]
    xbf = mem["xbf"]
    xc32 = mem["xc32"]
    sq = mem["sq"]
    r16 = mem["r16"]
    r2 = mem["r2"]
    ssum = mem["ssum"]
    sm = mem["smalls"]
    brep = mem["brep"]
    attO = mem["attO"]
    ptile = mem["ptile"]
    hbf0 = mem["hbf0"]
    wqkvsb = mem["wqkvsb"]
    k2 = mem["k2"]
    q2 = mem["q2"]
    qraw = mem["qraw"]
    vaug = mem["vaug"]
    h2c = mem["h2c"]
    prevcol = mem["prevcol"]
    hln = mem["hln"]
    pT = mem["pT"]
    w1sb = mem["w1sb"]
    w2sb = mem["w2sb"]
    wosb = mem["wosb"]
    xo32 = mem["xo32"]
    xcbf = mem["xcbf"]
    VW8 = 8 * 65

    def tok(n):
        return pre + n

    for dt in range(DT):
        P.dma("pool", wqkvsb[dt][:, :], W["wqkv_d"][dt * 128:(dt + 1) * 128, 0:1024],
              writes=[tok(f"wqkv{dt}")], slot="wqkv")
    for i in range(4):
        P.dma("pool", wosb[i][:, :], W["wo_d"][i * 128:(i + 1) * 128, :],
              writes=[tok(f"wo{i}")], slot="woS")

    for i in range(4):
        def z0(e, i=i):
            return e.memset(hbf0[i][:, 0:1], 0.0)
        P.op("dve", z0, writes=[tok(f"hb{i}")])

        def cp0(e, i=i):
            return e.tensor_copy(hbf0[i][:, 1:512], xbf[4 + i][:, 0:511])
        P.op("dve", cp0, reads=["xbf"], writes=[tok(f"hb{i}")])

    def hrhs(dt, ch):
        if dt < 4:
            return xbf[dt][:, ch * 512:(ch + 1) * 512], ["xbf"]
        if ch == 0:
            return hbf0[dt - 4][:, :], [tok(f"hb{dt-4}")]
        return xbf[dt][:, ch * 512 - 1:(ch + 1) * 512 - 1], ["xbf"]

    def bmul(e):
        return e.tensor_tensor(sm[0:1, 0:1][:, 0:1] if False else mem["qksc"][0:1, 0:64],
                               W["qkrow"][0:1, 0:64], W["qkrow"][0:1, 64:128], ALU.mult)
    P.op("dve", bmul, reads=[W["qkrow"].name], writes=[tok("qksc")])

    def bred(e):
        return e.tensor_reduce(sm[0:1, 46:47], mem["qksc"][0:1, 0:64], axis=AX,
                               op=ALU.max, apply_absolute_value=True)
    P.op("dve", bred, reads=[tok("qksc")], writes=[tok("bmx")])

    def bneg(e):
        return e.tensor_scalar(sm[0:1, 47:48], sm[0:1, 46:47], -8.0, -4.0, ALU.mult, ALU.add)
    P.op("dve", bneg, reads=[tok("bmx")], writes=[tok("nb")])
    P.dma("sp", bass.AP(dram["bounce"].tensor, 300, [[1, 1], [1, 1]]), sm[0:1, 47:48],
          reads=[tok("nb")], writes=[tok("dbB")], slot="dbB")
    P.dma("sp", brep[:, 0:1], bass.AP(dram["bounce"].tensor, 300, [[0, 128], [1, 1]]),
          reads=[tok("dbB")], writes=[tok("brep")], slot="brep")

    tsets = [
        {"qraw": (qraw, "qraw"), "sq": (sq, "sq"), "rf": (mem["rf32"], "rf32"),
         "xp1": (mem["xp1"], "xp1"), "xp2": (mem["xp2"], "xp2"), "bA": "a", "bB": "b"},
        {"qraw": (pT[20], "pT20"), "sq": (pT[21], "pT21"), "rf": (h2c[0], "h20"),
         "xp1": (pT[18], "pT18"), "xp2": (pT[19], "pT19"), "bA": "f", "bB": "g"},
    ]

    def l2chain(p4, ch, src_cols, dst, dsttok, which, ts):
        qr, qrt = ts["qraw"]
        sqb, sqt = ts["sq"]
        rf, rft = ts["rf"]
        x1b, x1t = ts["xp1"]
        x2b, x2t = ts["xp2"]
        bA, bB = ts["bA"], ts["bB"]

        def ev(e):
            return e.copy(qr[:, 0:512], ps[bA][:, :512])
        P.op("act", ev, reads=["ps_" + bA], writes=[tok(qrt)])

        def sqq(e):
            return e.activation(sqb[:, 0:512], qr[:, 0:512], AF.Square)
        P.op("act", sqq, reads=[tok(qrt)], writes=[tok(sqt)])

        def mmn(e):
            return e.matmul(ps[bB][:, :512], C["bd64"][:, :], sqb[:, 0:512], start=True, stop=True)
        P.op("pe", mmn, reads=[tok(sqt), "c"], writes=["ps_" + bB])

        def rln(e):
            return e.activation(rf[:, 0:512], ps[bB][:, :512], AF.Ln, bias=mem["eps18"][:, 0:1])
        P.op("act", rln, reads=["ps_" + bB, "eps18"], writes=[tok(rft)])

        def rex(e):
            return e.activation(rf[:, 0:512], rf[:, 0:512], AF.Exp, scale=-0.5)
        P.op("act", rex, reads=[tok(rft)], writes=[tok(rft)])

        def rcl(e):
            return e.tensor_scalar(x2b[:, 0:512], rf[:, 0:512], 1e12, None, ALU.min)
        P.op("dve", rcl, reads=[tok(rft)], writes=[tok(x2t)])

        def qh(e):
            return e.tensor_tensor(qr[:, 0:512], qr[:, 0:512], x2b[:, 0:512], ALU.mult)
        P.op("dve", qh, reads=[tok(qrt), tok(x2t)], writes=[tok(qrt)])

        def mrot(e):
            return e.matmul(ps[bB][:, :512], C["rot"][:, :], qr[:, 0:512], start=True, stop=True)
        P.op("pe", mrot, reads=[tok(qrt), "c"], writes=["ps_" + bB])
        scol = W["qscol"] if which == "q" else W["kscol"]
        srot = W["qsrotcol"] if which == "q" else W["ksrotcol"]

        def t1(e):
            return e.tensor_tensor(x1b[:, 0:512], qr[:, 0:512], mem["tcs"][:, :], ALU.mult)
        P.op("dve", t1, reads=[tok(qrt), tok("tcs")], writes=[tok(x1t)])

        def t1b(e, scol=scol):
            return e.tensor_scalar(x1b[:, 0:512], x1b[:, 0:512], scol[:, 0:1], None, ALU.mult)
        P.op("dve", t1b, reads=[tok(x1t), scol.name], writes=[tok(x1t)])

        def t2(e):
            return e.tensor_tensor(x2b[:, 0:512], ps[bB][:, :512], mem["tsn"][:, :], ALU.mult)
        P.op("dve", t2, reads=["ps_" + bB, tok("tsn")], writes=[tok(x2t)])

        def t2b(e, srot=srot):
            return e.tensor_scalar(x2b[:, 0:512], x2b[:, 0:512], srot[:, 0:1], None, ALU.mult)
        P.op("dve", t2b, reads=[tok(x2t), srot.name], writes=[tok(x2t)])

        def tsum(e, dst=dst, ch=ch):
            return e.tensor_tensor(dst, x1b[:, 0:512], x2b[:, 0:512], ALU.add)
        P.op("dve", tsum, reads=[tok(x1t), tok(x2t)], writes=[dsttok])

    for ch in range(NT):
        P.dma("sp", mem["tcs"][:, :], W["csk_d"][:, ch * 512:(ch + 1) * 512],
              writes=[tok("tcs")], slot="tcs")
        P.dma("sp", mem["tsn"][:, :], W["snk_d"][:, ch * 512:(ch + 1) * 512],
              writes=[tok("tsn")], slot="tsn")
        for p4 in range(4):
            ts = tsets[p4 % 2]

            def mmk(e, p4=p4, ch=ch, bA=ts["bA"]):
                last = None
                for dt in range(DT):
                    rhs, rtok = hrhs(dt, ch)
                    last = e.matmul(ps[bA][:, :512], wqkvsb[dt][:, 512 + p4 * 128:512 + (p4 + 1) * 128],
                                    rhs, start=(dt == 0), stop=(dt == DT - 1))
                return last
            rds = [tok(f"wqkv{dt}") for dt in range(DT)] + ["xbf"] + [tok(f"hb{i}") for i in range(4)]
            P.op("pe", mmk, reads=rds, writes=["ps_" + ts["bA"]])
            l2chain(p4, ch, (ch * 512, (ch + 1) * 512), k2[p4][:, ch * 512:(ch + 1) * 512],
                    tok(f"k2_{p4}"), "k", ts)

    for tt in range(16):
        def msv(e, tt=tt):
            return e.memset(vaug[tt][:, 0:VW8], 0.0)
        P.op("dve", msv, writes=[tok(f"va{tt}")])

        def onec(e, tt=tt):
            ap = bass.AP(vaug[tt], 64, [[vaug[tt].shape[1], 128], [65, 8], [1, 1]])
            return e.memset(ap, 1.0)
        P.op("dve", onec, reads=[tok(f"va{tt}")], writes=[tok(f"va{tt}")])
    for dt in range(DT):
        P.dma("pool", wqkvsb[dt][:, 512:1024], W["wqkv_d"][dt * 128:(dt + 1) * 128, 1024:1536],
              writes=[tok(f"wqkv{dt}")], slot="wqkv")
    for tt in range(16):
        ch = tt // 4
        def mmv(e, tt=tt, ch=ch):
            last = None
            for dt in range(DT):
                if dt < 4:
                    rhs = xbf[dt][:, tt * 128:(tt + 1) * 128]
                elif tt == 0:
                    rhs = hbf0[dt - 4][:, 0:128]
                else:
                    rhs = xbf[dt][:, tt * 128 - 1:(tt + 1) * 128 - 1]
                last = e.matmul(ps["a"][:, :512], rhs, wqkvsb[dt][:, 512:1024],
                                start=(dt == 0), stop=(dt == DT - 1))
            return last
        rds = [tok(f"wqkv{dt}") for dt in range(DT)] + ["xbf"] + [tok(f"hb{i}") for i in range(4)]
        P.op("pe", mmv, reads=rds, writes=["ps_a"])

        def evv(e, tt=tt):
            o = bass.AP(vaug[tt], 0, [[vaug[tt].shape[1], 128], [65, 8], [1, 64]])
            i = ps["a"].ap().rearrange("p (h c) -> p h c", c=64)[:, :, :]
            return e.copy(o, i)
        P.op("act", evv, reads=["ps_a"], writes=[tok(f"va{tt}")])

    for i in range(4):
        def zpc(e, i=i):
            return e.memset(prevcol[i][:, 0:1], 0.0)
        P.op("dve", zpc, writes=[tok(f"pc{i}")])

    def qph_ffn(w, all_set1=False):
        qlo = w * 512
        P.dma("sp", mem["tcs"][:, :], W["cs8q_d"][:, qlo:qlo + 512],
              writes=[tok("tcs")], slot="tcs")
        P.dma("sp", mem["tsn"][:, :], W["sn8q_d"][:, qlo:qlo + 512],
              writes=[tok("tsn")], slot="tsn")
        for p4 in range(4):
            ts = tsets[1] if all_set1 else tsets[p4 % 2]

            def mmq(e, p4=p4, w=w, bA=ts["bA"]):
                last = None
                for dt in range(DT):
                    rhs, _ = hrhs(dt, w)
                    last = e.matmul(ps[bA][:, :512], wqkvsb[dt][:, p4 * 128:(p4 + 1) * 128],
                                    rhs, start=(dt == 0), stop=(dt == DT - 1))
                return last
            rds = [tok(f"wqkv{dt}") for dt in range(DT)] + ["xbf"] + [tok(f"hb{i}") for i in range(4)]
            P.op("pe", mmq, reads=rds, writes=["ps_" + ts["bA"]])
            l2chain(p4, w, (qlo, qlo + 512), q2[p4][:, :], tok(f"q2_{p4}"), "q", ts)

    qph_ffn(0)
    for w in range(NT):
        qlo = w * 512
        for h in range(8):
            p4 = h // 2
            base = 64 * (h % 2)
            ktiles = list(range(max(0, 4 * w - 4), 4 * w + 4))
            nk = len(ktiles)
            for ki, t in enumerate(ktiles):
                pband = "c" if ki % 2 == 0 else "d"

                def mms(e, t=t, p4=p4, base=base, pband=pband):
                    return e.matmul(ps[pband][:, :512],
                                    k2[p4][base:base + 64, t * 128:(t + 1) * 128],
                                    q2[p4][base:base + 64, :], start=True, stop=True)
                P.op("pe", mms, reads=[tok(f"k2_{p4}"), tok(f"q2_{p4}")], writes=["ps_" + pband])
                pt = ptile[ki % 2]

                def evx(e, pband=pband, pt=pt):
                    return e.activation(pt[:, :], ps[pband][:, :512], AF.Exp, bias=brep[:, 0:1])
                P.op("act", evx, reads=["ps_" + pband, tok("brep")], writes=[tok(f"pt{ki%2}")])
                dj = t - 4 * w
                mk_ = C["diagmask"] if dj >= 0 else C["prevmask"]
                mj = dj if dj >= 0 else t - (4 * w - 4)

                def msk(e, mj=mj, mk_=mk_, pt=pt):
                    return e.tensor_tensor(pt[:, :], pt[:, :], mk_[:, mj * 512:(mj + 1) * 512], ALU.mult)
                P.op("dve", msk, reads=[tok(f"pt{ki%2}"), "c"], writes=[tok(f"pt{ki%2}")])

                def mpv(e, t=t, h=h, ki=ki, nk=nk):
                    return e.matmul(ps["e"][0:65, :512], vaug[t][:, h * 65:h * 65 + 65],
                                    ptile[ki % 2][:, :], start=(ki == 0), stop=(ki == nk - 1))
                P.op("pe", mpv, reads=[tok(f"va{t}"), tok(f"pt{ki%2}")], writes=["ps_e"])

            def evo(e, p4=p4, base=base):
                return e.copy(attO[p4][base:base + 64, :], ps["e"][0:64, :512])
            P.op("act", evo, reads=["ps_e"], writes=[tok(f"at{p4}")])
            sb = ssum[h % 2]

            def evs(e, sb=sb):
                return e.copy(sb[0:1, :], ps["e"][64:65, :512])
            P.op("act", evs, reads=["ps_e"], writes=[tok(f"ss{h%2}")])
            P.dma("sp", bass.AP(dram["sums"].tensor, h * 512, [[512, 1], [1, 512]]), sb[0:1, :],
                  reads=[tok(f"ss{h%2}")], writes=[tok("dS")], slot="dS")

        P.dma("sp", r16[0:8, :], bass.AP(dram["sums"].tensor, 0, [[512, 8], [1, 512]]),
              reads=[tok("dS")], writes=[tok("r16")], slot="r16")

        def lns(e):
            return e.activation(r16[0:8, :], r16[0:8, :], AF.Ln)
        P.op("act", lns, reads=[tok("r16")], writes=[tok("r16")])

        def exns(e):
            return e.activation(r16[0:8, :], r16[0:8, :], AF.Exp, scale=-1.0)
        P.op("act", exns, reads=[tok("r16")], writes=[tok("r16")])
        P.dma("sp", dram["r16d"][0:8, :], r16[0:8, :], reads=[tok("r16")], writes=[tok("dR")], slot="dR")
        for p4 in range(4):
            rp = r2[p4 % 2]
            P.dma("sp", rp[:, :], bass.AP(dram["r16d"].tensor, 0, [[512, 2], [1024, 4], [1, 512]])[:, p4, :],
                  reads=[tok("dR")], writes=[tok(f"r2{p4%2}")], slot=f"r2{p4%2}")

            def mrep(e, rp=rp):
                return e.matmul(ps["b"][:, :512], C["e2pair"][0:2, :], rp[0:2, :], start=True, stop=True)
            P.op("pe", mrep, reads=[tok(f"r2{p4%2}"), "c"], writes=["ps_b"])

            def nrm(e, p4=p4):
                return e.tensor_tensor(attO[p4][:, :], attO[p4][:, :], ps["b"][:, :512], ALU.mult)
            P.op("dve", nrm, reads=["ps_b", tok(f"at{p4}")], writes=[tok(f"at{p4}")])

        for dt in range(DT):
            P.dma(("sp", "act", "pool")[dt % 3], xc32[dt][:, :],
                  xres_in[dt * 128:(dt + 1) * 128, qlo:qlo + 512],
                  writes=[tok(f"xl{dt}")], slot=f"xl{dt}")

            def mmo(e, dt=dt):
                last = None
                for kt4 in range(4):
                    last = e.matmul(ps["a"][:, :512], wosb[kt4][:, dt * 128:(dt + 1) * 128],
                                    attO[kt4][:, :], start=(kt4 == 0), stop=(kt4 == 3))
                return last
            P.op("pe", mmo, reads=[tok(f"wo{i}") for i in range(4)] +
                 [tok(f"at{i}") for i in range(4)], writes=["ps_a"])

            def radd(e, dt=dt):
                return e.tensor_tensor(xc32[dt][:, :], ps["a"][:, :512], xc32[dt][:, :], ALU.add)
            P.op("dve", radd, reads=[tok(f"xl{dt}"), "ps_a"], writes=[tok(f"xl{dt}")])

        for i in range(4):
            def h2a(e, i=i):
                return e.tensor_copy(h2c[i][:, 0:1], prevcol[i][:, 0:1])
            P.op("dve", h2a, reads=[tok(f"pc{i}")], writes=[tok(f"h2{i}")])

            def h2b(e, i=i):
                return e.tensor_copy(h2c[i][:, 1:512], xc32[4 + i][:, 0:511])
            P.op("dve", h2b, reads=[tok(f"xl{4+i}")], writes=[tok(f"h2{i}")])
        for i in range(4):
            def spc(e, i=i):
                return e.tensor_copy(prevcol[i][:, 0:1], xc32[4 + i][:, 511:512])
            P.op("dve", spc, reads=[tok(f"xl{4+i}")], writes=[tok(f"pc{i}")])

        def h2tile(dt):
            if dt < 4:
                return xc32[dt], tok(f"xl{dt}")
            return h2c[dt - 4], tok(f"h2{dt-4}")

        for dt in range(DT):
            h2t, h2tok = h2tile(dt)

            def cbx(e, h2t=h2t):
                return e.tensor_copy(xcbf[:, :], h2t[:, :512])
            P.op("dve", cbx, reads=[h2tok], writes=[tok("xcbf")])

            def mmsx(e, dt=dt):
                return e.matmul(ps["f"][0:1, :512], C["ones128c"][:, 0:1], xcbf[:, :512],
                                start=(dt == 0), stop=(dt == DT - 1))
            P.op("pe", mmsx, reads=[tok("xcbf"), "c"], writes=["ps_f"])

            def sqx(e, h2t=h2t):
                return e.activation(sq[:, :512], h2t[:, :512], AF.Square)
            P.op("act", sqx, reads=[h2tok], writes=[tok("sq")])

            def mmsq(e, dt=dt):
                return e.matmul(ps["g"][0:1, :512], C["ones128c"][:, 0:1], sq[:, :512],
                                start=(dt == 0), stop=(dt == DT - 1))
            P.op("pe", mmsq, reads=[tok("sq"), "c"], writes=["ps_g"])
        statm, stats = mem["statm"], mem["stats"]
        statm2 = mem["rrep"]
        rrep, mrrep = mem["rrep"], mem["mrrep"]

        def st1(e):
            return e.tensor_scalar(statm[0:1, :], ps["f"][0:1, :512], 1.0 / D, None, ALU.mult)
        P.op("dve", st1, reads=["ps_f"], writes=[tok("stm")])

        def st2(e):
            return e.tensor_scalar(stats[0:1, :], ps["g"][0:1, :512], 1.0 / D, None, ALU.mult)
        P.op("dve", st2, reads=["ps_g"], writes=[tok("sts")])

        def st3(e):
            return e.tensor_tensor(statm2[0:1, :], statm[0:1, :], statm[0:1, :], ALU.mult)
        P.op("dve", st3, reads=[tok("stm")], writes=[tok("rrep")])

        def st4(e):
            return e.tensor_tensor(stats[0:1, :], stats[0:1, :], statm2[0:1, :], ALU.subtract)
        P.op("dve", st4, reads=[tok("sts"), tok("rrep")], writes=[tok("sts")])

        def st5(e):
            return e.activation(stats[0:1, :], stats[0:1, :], AF.Ln, bias=mem["eps11"][0:1, 0:1])
        P.op("act", st5, reads=[tok("sts"), "eps11"], writes=[tok("sts")])

        def st6(e):
            return e.activation(stats[0:1, :], stats[0:1, :], AF.Exp, scale=-0.5)
        P.op("act", st6, reads=[tok("sts")], writes=[tok("sts")])

        def st7(e):
            return e.tensor_tensor(statm[0:1, :], statm[0:1, :], stats[0:1, :], ALU.mult)
        P.op("dve", st7, reads=[tok("stm"), tok("sts")], writes=[tok("stm")])

        def mr1(e):
            return e.matmul(ps["f"][:, :512], C["ones1r"][0:1, :], stats[0:1, :], start=True, stop=True)
        P.op("pe", mr1, reads=[tok("sts"), "c"], writes=["ps_f"])

        def mr2(e):
            return e.matmul(ps["g"][:, :512], C["ones1r"][0:1, :], statm[0:1, :], start=True, stop=True)
        P.op("pe", mr2, reads=[tok("stm"), "c"], writes=["ps_g"])

        def evr(e):
            return e.copy(rrep[:, :], ps["f"][:, :512])
        P.op("act", evr, reads=["ps_f"], writes=[tok("rrep")])

        def evm(e):
            return e.copy(mrrep[:, :], ps["g"][:, :512])
        P.op("act", evm, reads=["ps_g"], writes=[tok("mrrep")])
        for dt in range(DT):
            h2t, h2tok = h2tile(dt)

            def a1(e, h2t=h2t):
                return e.tensor_tensor(mem["rf32"][:, :], h2t[:, :512], rrep[:, :], ALU.mult)
            P.op("dve", a1, reads=[h2tok, tok("rrep")], writes=[tok("rf32")])

            def a2(e):
                return e.tensor_tensor(mem["rf32"][:, :], mem["rf32"][:, :], mrrep[:, :], ALU.subtract)
            P.op("dve", a2, reads=[tok("rf32"), tok("mrrep")], writes=[tok("rf32")])

            def a3(e, dt=dt):
                return e.tensor_scalar(hln[dt][:, :], mem["rf32"][:, :],
                                       W["lgsb"][:, dt:dt + 1], W["lbsb"][:, dt:dt + 1],
                                       ALU.mult, ALU.add)
            P.op("dve", a3, reads=[tok("rf32"), W["lgsb"].name, W["lbsb"].name],
                 writes=[tok(f"hln{dt}")])

        if w + 1 < NT:
            qph_ffn(w + 1, all_set1=True)
        for j in range(22):
            buf = j % 2
            for gidx in range(2):
                wsrc = bass.AP(W["w1_d"].tensor, ((j * 2 + gidx) * 128) * 1024,
                               [[1024, 128], [1, 1024]])
                P.dma("pool", w1sb[buf][:, gidx * 1024:(gidx + 1) * 1024],
                      wsrc, writes=[tok(f"w1b{buf}")], slot=f"w1b{buf}")
            def mma(e, j=j, buf=buf):
                last = None
                for dt in range(DT):
                    last = e.matmul(ps["c"][:, :512], w1sb[buf][:, dt * 256:dt * 256 + 128],
                                    hln[dt][:, :], start=(dt == 0), stop=(dt == DT - 1))
                return last
            P.op("pe", mma, reads=[tok(f"w1b{buf}")] + [tok(f"hln{d}") for d in range(DT)],
                 writes=["ps_c"])

            def mmg(e, j=j, buf=buf):
                last = None
                for dt in range(DT):
                    last = e.matmul(ps["d"][:, :512], w1sb[buf][:, dt * 256 + 128:dt * 256 + 256],
                                    hln[dt][:, :], start=(dt == 0), stop=(dt == DT - 1))
                return last
            P.op("pe", mmg, reads=[tok(f"w1b{buf}")] + [tok(f"hln{d}") for d in range(DT)],
                 writes=["ps_d"])

            def eva(e, j=j):
                return e.activation(qraw[:, :], ps["c"][:, :512], AF.Identity,
                                    bias=W["b1sb"][:, 2 * j:2 * j + 1])
            P.op("act", eva, reads=["ps_c", W["b1sb"].name], writes=[tok("qraw")])

            def evg(e, j=j):
                return e.activation(sq[:, :512], ps["d"][:, :512], AF.Gelu,
                                    bias=W["b1sb"][:, 2 * j + 1:2 * j + 2])
            P.op("act", evg, reads=["ps_d", W["b1sb"].name], writes=[tok("sq")])

            def pmul(e, j=j):
                return e.tensor_tensor(pT[j][:, :], qraw[:, :], sq[:, :512], ALU.mult)
            P.op("dve", pmul, reads=[tok("qraw"), tok("sq")], writes=[tok(f"pT{j}")])

        banks = ["a", "b", "c", "d", "e", "f", "g", "h"]
        for j in range(22):
            buf = j % 2
            P.dma("pool", w2sb[buf][:, :], W["w2_d"][j * 128:(j + 1) * 128, :],
                  writes=[tok(f"w2b{buf}")], slot=f"w2b{buf}")
            for dt in range(DT):
                def mmw2(e, j=j, buf=buf, dt=dt):
                    return e.matmul(ps[banks[dt]][:, :512], w2sb[buf][:, dt * 128:(dt + 1) * 128],
                                    pT[j][:, :], start=(j == 0), stop=(j == 21))
                P.op("pe", mmw2, reads=[tok(f"w2b{buf}"), tok(f"pT{j}")], writes=["ps_" + banks[dt]])

        for dt in range(DT):
            def fadd(e, dt=dt):
                return e.scalar_tensor_tensor(xc32[dt][:, :], ps[banks[dt]][:, :512],
                                              W["b2sb"][:, dt:dt + 1], xc32[dt][:, :],
                                              ALU.add, ALU.add)
            P.op("dve", fadd, reads=[tok(f"xl{dt}"), "ps_" + banks[dt], W["b2sb"].name],
                 writes=[tok(f"xl{dt}")])
        if opts.get("last"):
            for dt in range(DT):
                xob = xo32[dt % 2]
                P.dma(("sp", "act")[dt % 2], xob[:, :],
                      opts["xorig"][dt * 128:(dt + 1) * 128, qlo:qlo + 512],
                      writes=[tok(f"xo{dt % 2}")], slot=f"xo{dt % 2}")

                def toradd(e, dt=dt, xob=xob):
                    return e.tensor_tensor(xc32[dt][:, :], xc32[dt][:, :], xob[:, :], ALU.add)
                P.op("dve", toradd, reads=[tok(f"xl{dt}"), tok(f"xo{dt % 2}")],
                     writes=[tok(f"xl{dt}")])
            emit_postln(P, mem, C, tok, xc32, xbf, qlo, xres_out, pre)
        else:
            for dt in range(DT):
                P.dma(("sp", "act", "pool")[dt % 3],
                      xres_out[dt * 128:(dt + 1) * 128, qlo:qlo + 512], xc32[dt][:, :],
                      reads=[tok(f"xl{dt}")], writes=[pre + f"dxo{dt}"], slot=f"xdo{dt}")


def alloc_ffn_buffers(alloc, sh):
    mem = {}
    mem["xbf"] = sh["xbf"]
    mem["hbf0"] = [alloc(f"hbf0_{i}", [128, 512], BF16) for i in range(4)]
    mem["wqkvsb"] = [alloc(f"wqkv{i}", [128, 1024], BF16) for i in range(DT)]
    mem["k2"] = [alloc(f"k2_{i}", [128, S], BF16) for i in range(4)]
    mem["q2"] = [alloc(f"q2_{i}", [128, 512], BF16) for i in range(4)]
    mem["qraw"] = alloc("qraw", [128, 512], BF16)
    mem["rf32"] = alloc("rf32", [128, 512], F32)
    mem["xp1"] = alloc("xp1", [128, 512], BF16)
    mem["xp2"] = alloc("xp2", [128, 512], BF16)
    mem["tcs"] = alloc("tcs", [128, 512], BF16)
    mem["tsn"] = alloc("tsn", [128, 512], BF16)
    mem["vaug"] = [alloc(f"vaug{i}", [128, 8 * 65], BF16) for i in range(16)]
    mem["attO"] = [alloc(f"attO{i}", [128, 512], BF16) for i in range(4)]
    mem["xc32"] = [alloc(f"xc32_{i}", [128, 512], F32) for i in range(DT)]
    mem["h2c"] = [alloc(f"h2c{i}", [128, 512], F32) for i in range(4)]
    mem["prevcol"] = [alloc(f"pvc{i}", [128, 1], F32) for i in range(4)]
    mem["hln"] = [alloc(f"hln{i}", [128, 512], BF16) for i in range(DT)]
    mem["pT"] = [alloc(f"pT{i}", [128, 512], BF16) for i in range(22)]
    mem["w1sb"] = [alloc(f"w1sb{i}", [128, 2048], BF16) for i in range(2)]
    mem["w2sb"] = [alloc(f"w2sb{i}", [128, 1024], BF16) for i in range(2)]
    mem["wosb"] = [alloc(f"wosb{i}", [128, 1024], BF16) for i in range(4)]
    mem["xo32"] = [alloc(f"xo32_{i}", [128, 512], F32) for i in range(2)]
    mem["sq"] = alloc("sq", [128, 512], BF16)
    mem["xcbf"] = alloc("xcbf", [128, 512], BF16)
    mem["smalls"] = alloc("smalls", [128, 48], F32)
    mem["r16"] = alloc("r16", [16, 512], F32)
    mem["r2"] = [alloc(f"r2_{i}", [2, 512], F32) for i in range(2)]
    mem["ssum"] = [t for t in mem["r2"]]
    mem["brep"] = alloc("brep", [128, 1], F32)
    mem["statm"] = alloc("statm", [1, 512], F32)
    mem["stats"] = alloc("stats", [1, 512], F32)
    mem["eps11"] = sh["eps11"]
    mem["eps18"] = sh["eps18"]
    mem["qksc"] = alloc("qksc", [1, 64], F32)
    mem["rrep"] = alloc("rrep", [128, 512], F32)
    mem["mrrep"] = alloc("mrrep", [128, 512], F32)
    mem["ptile"] = [alloc(f"ptile{i}", [128, 512], BF16) for i in range(2)]
    mem["psum"] = sh["psum"]
    return mem


def prep_ffn_host_weights(wqkv, qsc, ksc, wo, lg, lb, w1, b1, w2, b2):
    HID = 2730
    NB = 22
    d = {}
    d["wqkv"] = f32(wqkv)
    d["wo"] = f32(wo)
    w1p = np.zeros((1024, NB * 256), np.float32)
    b1p = np.zeros((128, 2 * NB), np.float32)
    for j in range(NB):
        n = min(128, HID - j * 128)
        w1p[:, j * 256:j * 256 + n] = w1[:, j * 128:j * 128 + n]
        w1p[:, j * 256 + 128:j * 256 + 128 + n] = w1[:, HID + j * 128:HID + j * 128 + n]
        b1p[:n, 2 * j] = b1[j * 128:j * 128 + n]
        b1p[:n, 2 * j + 1] = b1[HID + j * 128:HID + j * 128 + n]
    d["w1p"] = np.ascontiguousarray(
        w1p.reshape(2, 4, 128, NB, 256).transpose(3, 0, 2, 1, 4).reshape(NB * 2 * 128, 1024))
    d["b1p"] = b1p
    w2p = np.zeros((NB * 128, 1024), np.float32)
    w2p[:HID] = w2
    d["w2p"] = w2p
    d["b2"] = f32(b2.reshape(8, 128).T)
    d["lg"] = f32(lg.reshape(8, 128).T)
    d["lb"] = f32(lb.reshape(8, 128).T)
    qkrow = np.zeros((1, 128), np.float32)
    qkrow[0, :64] = qsc; qkrow[0, 64:128] = ksc
    d["qkrow"] = qkrow
    rot = lambda v: np.concatenate([v[32:], v[:32]])
    d["qscol"] = f32(np.concatenate([qsc, qsc]).reshape(128, 1))
    d["qsrotcol"] = f32(np.concatenate([rot(qsc), rot(qsc)]).reshape(128, 1))
    d["kscol"] = f32(np.concatenate([ksc, ksc]).reshape(128, 1))
    d["ksrotcol"] = f32(np.concatenate([rot(ksc), rot(ksc)]).reshape(128, 1))
    return d



def emit_quant(P, mem, tok, src, src_tok, inv, inv_tok, scl, scl_tok, out, out_tok, W512, nb):
    q = mem["q_"]

    def bview(t):
        return t[:, 0:nb].unsqueeze(2).broadcast_to([128, nb, 16])

    def rs(t):
        return t.ap().rearrange("p (b c) -> p b c", c=16)[:, 0:nb, :]

    def sg(e):
        return e.sign(q["sg"][:, :W512], src)
    P.op("act", sg, reads=[src_tok], writes=[tok("qsg")])

    def ab(e):
        return e.activation(q["ya"][:, :W512], src, AF.Abs)
    P.op("act", ab, reads=[src_tok], writes=[tok("qya")])

    def ymul(e):
        return e.tensor_tensor(rs(q["ya"]), rs(q["ya"]), bview(inv), ALU.mult)
    P.op("dve", ymul, reads=[tok("qya"), inv_tok], writes=[tok("qya")])

    y = q["ya"]

    def c0(e):
        return e.tensor_scalar(q["fB"][:, :W512], y[:, :W512], 0.25, None, ALU.is_gt)
    P.op("dve", c0, reads=[tok("qya")], writes=[tok("qfB")])
    for thr in (0.75, 1.25, 1.75):
        def ci(e, thr=thr):
            return e.scalar_tensor_tensor(q["fB"][:, :W512], y[:, :W512], thr,
                                          q["fB"][:, :W512], ALU.is_gt, ALU.add)
        P.op("dve", ci, reads=[tok("qya"), tok("qfB")], writes=[tok("qfB")])

    def half(e):
        return e.tensor_scalar(q["fB"][:, :W512], q["fB"][:, :W512], 0.5, None, ALU.mult)
    P.op("dve", half, reads=[tok("qfB")], writes=[tok("qfB")])
    for thr in (2.5, 3.5, 5.0, 5.0):
        def di(e, thr=thr):
            return e.scalar_tensor_tensor(q["fB"][:, :W512], y[:, :W512], thr,
                                          q["fB"][:, :W512], ALU.is_gt, ALU.add)
        P.op("dve", di, reads=[tok("qya"), tok("qfB")], writes=[tok("qfB")])

    def h1(e):
        return e.tensor_tensor(rs(q["fB"]), rs(q["fB"]), bview(scl), ALU.mult)
    P.op("dve", h1, reads=[tok("qfB"), scl_tok], writes=[tok("qfB")])

    def h2(e):
        return e.tensor_tensor(out, q["fB"][:, :W512], q["sg"][:, :W512], ALU.mult)
    P.op("dve", h2, reads=[tok("qfB"), tok("qsg")], writes=[out_tok])


def emit_l1ffn(P, mem, W, opts):
    pre = opts["pre"]
    dram = opts["dram"]
    C = W["consts"]
    ps = mem["psum"]
    psh = mem["psumh"]
    xc32 = mem["xc32"]

    def tok(n):
        return pre + n

    NB22 = 22

    def qslice(it, wsrcs, spills, buf):
        wn = mem["wnat"][buf]
        c0 = 0
        widths = []
        for qi, (dram_ap, wdt) in enumerate(wsrcs):
            P.dma(("sp", "act")[qi % 2], wn[:, c0:c0 + wdt],
                  dram_ap[it * 128:(it + 1) * 128, :],
                  writes=[tok(f"wn{buf}")], slot=f"wn{buf}")
            widths.append(wdt)
            c0 += wdt
        Wt = c0
        nb = Wt // 16

        def bmx(e, wn=wn, nb=nb):
            return e.tensor_reduce(mem["bm"][:, 0:nb],
                                   wn.ap().rearrange("p (b c) -> p b c", c=16)[:, 0:nb, :],
                                   axis=AX, op=ALU.max, apply_absolute_value=True)
        P.op("dve", bmx, reads=[tok(f"wn{buf}")], writes=[tok("bm")])

        def mkscl(e, nb=nb):
            return e.tensor_scalar(mem["scl"][:, 0:nb], mem["bm"][:, 0:nb], 1.0 / 6.0, 1e-12,
                                   ALU.mult, ALU.max)
        P.op("dve", mkscl, reads=[tok("bm")], writes=[tok("scl")])

        def mkinv(e, nb=nb):
            return e.reciprocal(mem["inv"][:, 0:nb], mem["scl"][:, 0:nb])
        P.op("dve", mkinv, reads=[tok("scl")], writes=[tok("inv")])
        emit_quant(P, mem, tok,
                   wn[:, 0:Wt], tok(f"wn{buf}"),
                   mem["inv"][:, :], tok("inv"),
                   mem["scl"][:, :], tok("scl"),
                   mem["wqn"][:, 0:Wt], tok("wqn"), Wt, nb)
        nblk = Wt // 128
        wq0 = (it % 2) * 512

        def trp(e, nblk=nblk):
            last = None
            for b in range(nblk):
                last = e.transpose(psh[:, b * 128:(b + 1) * 128],
                                   mem["wqn"][:, b * 128:(b + 1) * 128],
                                   C["ident"][:, :])
            return last
        P.op("pe", trp, reads=[tok("wqn"), "c"], writes=["ps_h"])

        def evt(e, nblk=nblk, wq0=wq0):
            return e.copy(mem["wqt"][:, wq0:wq0 + nblk * 128], psh[:, 0:nblk * 128])
        P.op("act", evt, reads=["ps_h"], writes=[tok(f"wqt{it % 2}")])
        b = 0
        for qi, (spill_t, sw) in enumerate(spills):
            nb2 = widths[qi] // 128
            dst = bass.AP(spill_t, it * 128, [[sw, 128], [128 * sw, nb2], [1, 128]])
            src = mem["wqt"][:, wq0 + b * 128:wq0 + (b + nb2) * 128].rearrange(
                "p (g c) -> p g c", c=128)
            P.dma("sp", dst, src, reads=[tok(f"wqt{it % 2}")],
                  writes=[tok("sp" + "gud"[qi] if len(spills) > 1 else "spd")], slot="spw")
            b += nb2

    for it in range(NB22):
        qslice(it, [(W["wgs_d"], 128), (W["wus_d"], 128)],
               [(dram["in_g"].tensor, 2816), (dram["in_u"].tensor, 2816)], it % 2)
    for it in range(8):
        qslice(it, [(W["wds_d"], 384)], [(dram["in_d"].tensor, 1024)], it % 2)

    rg = [[0, 1, 2, 3, 4, 5, 6, 7]]
    for nm, src_d, dst_d, rtok in [("g", "in_g", "qwg", "spg"), ("u", "in_u", "qwu", "spu"),
                                   ("d", "in_d", "qwd", "spd")]:
        def agx(e, src_d=src_d, dst_d=dst_d):
            return e.collective_compute("AllGather", ALU.bypass,
                                        ins=[dram[src_d]], outs=[dram[dst_d]],
                                        replica_groups=rg)
        P.op("pool", agx, reads=[tok(rtok)], writes=[tok("ag" + nm)])


    banks = ["a", "b", "c", "d", "e", "f", "g", "h"]
    for chk in range(NT):
        qlo = chk * 512
        for dt in range(DT):
            P.dma(("sp", "act", "pool")[dt % 3], xc32[dt][:, :],
                  opts["xres_in"][dt * 128:(dt + 1) * 128, qlo:qlo + 512],
                  writes=[tok(f"xl{dt}")], slot=f"xl{dt}")
        for tt in range(4):
            for half in range(2):
                def trx(e, tt=tt, half=half):
                    last = None
                    for d4 in range(4):
                        dt = half * 4 + d4
                        last = e.transpose(ps["a"][:, d4 * 128:(d4 + 1) * 128],
                                           xc32[dt][:, tt * 128:(tt + 1) * 128],
                                           C["identf"][:, :])
                    return last
                P.op("pe", trx, reads=[tok(f"xl{half*4+i}") for i in range(4)] + ["c"],
                     writes=["ps_a"])

                def evx(e, tt=tt, half=half):
                    return e.copy(mem["xnat"][tt % 2][:, half * 512:(half + 1) * 512], ps["a"][:, 0:512])
                P.op("act", evx, reads=["ps_a"], writes=[tok(f"xn{tt % 2}")])

            def bmx2(e, tt=tt):
                return e.tensor_reduce(mem["bm"][:, 0:64],
                                       mem["xnat"][tt % 2].ap().rearrange("p (b c) -> p b c", c=16),
                                       axis=AX, op=ALU.max, apply_absolute_value=True)
            P.op("dve", bmx2, reads=[tok(f"xn{tt % 2}")], writes=[tok("bm")])

            def mkscl2(e):
                return e.tensor_scalar(mem["scl"][:, 0:64], mem["bm"][:, 0:64], 1.0 / 6.0, 1e-12,
                                       ALU.mult, ALU.max)
            P.op("dve", mkscl2, reads=[tok("bm")], writes=[tok("scl")])

            def mkinv2(e):
                return e.reciprocal(mem["inv"][:, 0:64], mem["scl"][:, 0:64])
            P.op("dve", mkinv2, reads=[tok("scl")], writes=[tok("inv")])
            emit_quant(P, mem, tok,
                       mem["xnat"][tt % 2][:, 0:1024], tok(f"xn{tt % 2}"),
                       mem["inv"][:, :], tok("inv"),
                       mem["scl"][:, :], tok("scl"),
                       mem["xqn"][:, 0:1024], tok("xqn"), 1024, 64)

            def trq(e):
                last = None
                for b in range(8):
                    last = e.transpose(psh[:, b * 128:(b + 1) * 128],
                                       mem["xqn"][:, b * 128:(b + 1) * 128], C["ident"][:, :])
                return last
            P.op("pe", trq, reads=[tok("xqn"), "c"], writes=["ps_h"])

            def evq(e, tt=tt):
                o = bass.AP(mem["xqT"], tt * 128, [[4096, 128], [512, 8], [1, 128]])
                return e.copy(o, psh.ap().rearrange("p (g c) -> p g c", c=128))
            P.op("act", evq, reads=["ps_h"], writes=[tok("xqT")])

        def emit_trh(ncc):
            nw = min(512, 2816 - ncc * 512)
            nblk = nw // 128
            for tt in range(4):
                def trh(e, tt=tt, nblk=nblk, nw=nw):
                    last = None
                    for b in range(nblk):
                        last = e.transpose(psh[:, b * 128:(b + 1) * 128],
                                           mem["hqn"][:, tt * nw + b * 128:tt * nw + (b + 1) * 128],
                                           C["ident"][:, :])
                    return last
                P.op("pe", trh, reads=[tok("hqn"), "c"], writes=["ps_h"])

                def evh(e, tt=tt, nblk=nblk, ncc=ncc):
                    o = bass.AP(mem["hqT"], (ncc * 4) * 512 + tt * 128,
                                [[22 * 512, 128], [512, nblk], [1, 128]])
                    return e.copy(o, psh.ap().rearrange("p (g c) -> p g c", c=128)[:, 0:nblk, :])
                P.op("act", evh, reads=["ps_h"], writes=[tok("hqT")])

        for ncc in range(6):
            n0 = ncc * 512
            nw = min(512, 2816 - n0)
            wset = ncc % 2
            for gi, nmW in enumerate(["qwg", "qwu"]):
                for dt in range(DT):
                    wbuf = mem["wstr"][wset][gi][dt]
                    wsrc = bass.AP(dram[nmW].tensor, (dt * 128) * 2816 + n0,
                                   [[2816, 128], [1, nw]])
                    P.dma("pool", wbuf[:, 0:nw], wsrc,
                          reads=[tok("ag" + "gu"[gi])],
                          writes=[tok(f"ws{wset}_{gi}_{dt}")], slot=f"ws{gi}_{dt}")
            for tt in range(4):
                for gi, bank in [(0, "c"), (1, "d")]:
                    def mmg(e, tt=tt, gi=gi, bank=bank, nw=nw, wset=wset):
                        last = None
                        for dt in range(DT):
                            last = e.matmul(ps[bank][:, 0:nw],
                                            mem["xqT"][:, dt * 512 + tt * 128:dt * 512 + (tt + 1) * 128],
                                            mem["wstr"][wset][gi][dt][:, 0:nw],
                                            start=(dt == 0), stop=(dt == DT - 1))
                        return last
                    P.op("pe", mmg,
                         reads=[tok("xqT")] + [tok(f"ws{wset}_{gi}_{d}") for d in range(DT)],
                         writes=["ps_" + bank])

                def slu(e, nw=nw):
                    return e.activation(mem["su"][:, 0:nw], ps["c"][:, 0:nw], AF.Silu)
                P.op("act", slu, reads=["ps_c"], writes=[tok("su")])

                def hmul(e, nw=nw, tt=tt):
                    return e.tensor_tensor(mem["hm4"][:, tt * nw:(tt + 1) * nw],
                                           mem["su"][:, 0:nw], ps["d"][:, 0:nw], ALU.mult)
                P.op("dve", hmul, reads=[tok("su"), "ps_d"], writes=[tok("hm4")])

            if ncc > 0:
                emit_trh(ncc - 1)
            HB = 4 * nw
            nb = HB // 16

            def bmx3(e, HB=HB, nb=nb):
                return e.tensor_reduce(mem["bm"][:, 0:nb],
                                       mem["hm4"].ap().rearrange("p (b c) -> p b c", c=16)[:, 0:nb, :],
                                       axis=AX, op=ALU.max, apply_absolute_value=True)
            P.op("dve", bmx3, reads=[tok("hm4")], writes=[tok("bm")])

            def mkscl3(e, nb=nb):
                return e.tensor_scalar(mem["scl"][:, 0:nb], mem["bm"][:, 0:nb], 1.0 / 6.0, 1e-12,
                                       ALU.mult, ALU.max)
            P.op("dve", mkscl3, reads=[tok("bm")], writes=[tok("scl")])

            def mkinv3(e, nb=nb):
                return e.reciprocal(mem["inv"][:, 0:nb], mem["scl"][:, 0:nb])
            P.op("dve", mkinv3, reads=[tok("scl")], writes=[tok("inv")])
            emit_quant(P, mem, tok,
                       mem["hm4"][:, 0:HB], tok("hm4"),
                       mem["inv"][:, :], tok("inv"),
                       mem["scl"][:, :], tok("scl"),
                       mem["hqn"][:, 0:HB], tok("hqn"), HB, nb)
        emit_trh(5)

        for half in range(2):
            for j in range(NB22):
                wbuf = mem["w2str"][j % 2]
                src = bass.AP(dram["qwd"].tensor, (j * 128) * 1024 + half * 512,
                              [[1024, 128], [1, 512]])
                P.dma("pool", wbuf[:, 0:512], src, reads=[tok("agd")],
                      writes=[tok(f"wd{j%2}")], slot=f"wd{j%2}")
                for d4 in range(4):
                    dt = half * 4 + d4

                    def mmd(e, j=j, d4=d4):
                        return e.matmul(ps[banks[d4]][:, 0:512],
                                        mem["w2str"][j % 2][:, d4 * 128:(d4 + 1) * 128],
                                        mem["hqT"][:, j * 512:j * 512 + 512],
                                        start=(j == 0), stop=(j == NB22 - 1))
                    P.op("pe", mmd, reads=[tok(f"wd{j%2}"), tok("hqT")], writes=["ps_" + banks[d4]])
            for d4 in range(4):
                dt = half * 4 + d4

                def fadd(e, dt=dt, d4=d4):
                    return e.tensor_tensor(xc32[dt][:, :], ps[banks[d4]][:, 0:512], xc32[dt][:, :],
                                           ALU.add)
                P.op("dve", fadd, reads=[tok(f"xl{dt}"), "ps_" + banks[d4]], writes=[tok(f"xl{dt}")])
        emit_postln(P, mem, C, tok, xc32, opts.get("xbf"), qlo, opts["xres_out"], pre)


def alloc_l1ffn_buffers(alloc, sh):
    mem = {}
    mem["xc32"] = [alloc(f"xc32_{i}", [128, 512], F32) for i in range(DT)]
    mem["wnat"] = [alloc(f"wnat{i}", [128, 512], F32) for i in range(2)]
    mem["bm"] = alloc("bm", [128, 176], F32)
    mem["scl"] = alloc("scl", [128, 176], F32)
    mem["inv"] = alloc("inv", [128, 176], F32)
    mem["wqn"] = alloc("wqn", [128, 512], BF16)
    mem["wqt"] = alloc("wqt", [128, 1024], BF16)
    mem["q_"] = {k: alloc("q_" + k, [128, 2048], F32 if k == "ya" else BF16)
                 for k in ["sg", "ya", "fB"]}
    mem["xnat"] = [alloc(f"xnat{i}", [128, 1024], F32) for i in range(2)]
    mem["xqn"] = alloc("xqn", [128, 1024], BF16)
    mem["xqT"] = alloc("xqT", [128, 8 * 512], BF16)
    mem["wstr"] = [[[alloc(f"wstr{s}_{g}_{d}", [128, 512], BF16) for d in range(DT)]
                    for g in range(2)] for s in range(2)]
    mem["su"] = alloc("su", [128, 512], F32)
    mem["hm4"] = alloc("hm4", [128, 2048], F32)
    mem["hqn"] = alloc("hqn", [128, 2048], BF16)
    mem["hqT"] = alloc("hqT", [128, 22 * 512], BF16)
    mem["w2str"] = [alloc(f"w2str{i}", [128, 1024], BF16) for i in range(2)]
    mem["sq"] = alloc("sq", [128, 512], BF16)
    mem["xcbf"] = alloc("xcbf", [128, 512], BF16)
    mem["statm"] = alloc("statm", [1, 512], F32)
    mem["stats"] = alloc("stats", [1, 512], F32)
    mem["rrep"] = alloc("rrep", [128, 512], F32)
    mem["mrrep"] = alloc("mrrep", [128, 512], F32)
    mem["eps11"] = sh["eps11"]
    mem["psum"] = sh["psum"]
    mem["psumh"] = sh["psumh"]
    return mem



FFN_SMALLS = ["b1p", "b2", "lg", "lb", "qkrow", "qscol", "qsrotcol", "kscol", "ksrotcol"]
FFN_WSHAPES = [("wqkv", [1024, 1536]), ("wo", [512, 1024]), ("w1p", [22 * 2 * 128, 1024]),
               ("b1p", [128, 44]), ("w2p", [22 * 128, 1024]), ("b2", [128, 8]),
               ("lg", [128, 8]), ("lb", [128, 8]), ("qkrow", [1, 128]),
               ("qscol", [128, 1]), ("qsrotcol", [128, 1]), ("kscol", [128, 1]),
               ("ksrotcol", [128, 1])]
CONST_LIST = ["ones128c", "ones1r", "bd64", "e2pair", "diagmask", "diagneg", "slb",
              "prevmask", "rot", "ident", "identf"]
GLOBAL_CONSTS = ["ones128c", "ones1r", "bd64", "e2pair", "diagmask"]


def build_mega_program(n_stages=4, debug_mid=False):
    nc = bass.Bass("TRN2", target_bir_lowering=False)
    consts_np = {**build_attn_consts(), **build_ffn_consts()}
    ident = np.eye(128, dtype=np.float32)
    consts_np["ident"] = to_bf16(ident)
    consts_np["identf"] = f32(ident)
    est = contextlib.ExitStack()
    nc._est = est

    xin = nc.dram_tensor("xT", [D, S], F32, kind="ExternalInput")
    xout = nc.dram_tensor("xoutT", [D, S], F32, kind="ExternalOutput")
    midkind = "ExternalOutput" if debug_mid else "Internal"
    x1 = nc.dram_tensor("x1T", [D, S], F32, kind=midkind)
    xm = nc.dram_tensor("xmT", [D, S], F32, kind="Internal")
    x2 = nc.dram_tensor("x2T", [D, S], F32, kind=midkind)
    x3 = nc.dram_tensor("x3T", [D, S], F32, kind=midkind)

    dram = {
        "bounce": nc.dram_tensor("scr_bounce", [1, 512], F32, kind="Internal").ap(),
        "sums": nc.dram_tensor("scr_sums", [1, 16 * 512], F32, kind="Internal").ap(),
        "r16d": nc.dram_tensor("scr_r16", [16, 512], F32, kind="Internal").ap(),
        "qwg": nc.dram_tensor("scr_qwg", [1024, 2816], BF16, kind="Internal").ap(),
        "qwu": nc.dram_tensor("scr_qwu", [1024, 2816], BF16, kind="Internal").ap(),
        "qwd": nc.dram_tensor("scr_qwd", [3072, 1024], BF16, kind="Internal").ap(),
        "in_g": nc.dram_tensor("scr_in_g", [128, 2816], BF16, kind="Internal").ap(),
        "in_u": nc.dram_tensor("scr_in_u", [128, 2816], BF16, kind="Internal").ap(),
        "in_d": nc.dram_tensor("scr_in_d", [384, 1024], BF16, kind="Internal").ap(),
    }
    for nm in ["cs8q", "sn8q", "csk", "snk"]:
        dram[nm] = nc.dram_tensor("c_" + nm, [128, S], BF16, kind="ExternalInput").ap()
    vhld = nc.dram_tensor("c_vhl", [2, 16 * 512], BF16, kind="ExternalInput")

    a0 = {nm: nc.dram_tensor("a0_" + nm, [D, D], F32, kind="ExternalInput")
          for nm in ["Wq", "Wk", "Wv", "Wo"]}
    a0b = {nm: nc.dram_tensor("a0_" + nm, [128, 8], F32, kind="ExternalInput")
           for nm in ["bq", "bo"]}
    fd = {}
    for L in range(2):
        for nm, shp in FFN_WSHAPES:
            fd[(L, nm)] = nc.dram_tensor(f"f{L}_{nm}", shp, F32, kind="ExternalInput")
    a1 = {nm: nc.dram_tensor("a1_" + nm, [D, D], F32, kind="ExternalInput")
          for nm in ["WqT", "WkT", "WvT", "WoT"]}
    a1b = {nm: nc.dram_tensor("a1_" + nm, [128, 8], F32, kind="ExternalInput")
           for nm in ["bq", "bk", "bv", "bo"]}
    qd = {"wgs": nc.dram_tensor("q_wgs", [2816, 128], F32, kind="ExternalInput"),
          "wus": nc.dram_tensor("q_wus", [2816, 128], F32, kind="ExternalInput"),
          "wds": nc.dram_tensor("q_wds", [1024, 384], F32, kind="ExternalInput")}

    P = Prog(nc)
    sh = make_shared(nc, est)

    def gal(n, s, d):
        return est.enter_context(nc.sbuf_tensor(n, s, d))
    CG = load_consts(P, nc, gal, consts_np, GLOBAL_CONSTS)

    def mseps(e):
        return e.memset(sh["eps11"][0:1, 0:1], 1e-5)
    P.op("dve", mseps, writes=["eps11"])

    def mseps2(e):
        return e.memset(sh["eps18"][:, 0:1], 1e-18)
    P.op("dve", mseps2, writes=["eps18"])

    for dt in range(DT):
        P.dma("pool", sh["xbf"][dt][:, :], xin.ap()[dt * 128:(dt + 1) * 128, :],
              writes=["xbf"], slot="xbf")

    ar = nc.auto_arena()
    al = lambda n, s, d: ar.sbuf_tensor("s1_" + n, s, d)
    mem = alloc_attn_buffers(al, sh)
    C = {**CG, **load_consts(P, nc, al, consts_np, ["diagneg", "slb"])}
    W = {"consts": C, "slb": C["slb"], "vhl_d": vhld.ap()}
    W["setA"] = [al(f"wsetA{i}", [128, D], BF16) for i in range(DT)]
    W["setB"] = [al(f"wsetB{i}", [128, D], BF16) for i in range(DT)]
    W["wq_d"] = a0["Wq"].ap(); W["wk_d"] = a0["Wk"].ap()
    W["wv_d"] = a0["Wv"].ap(); W["wo_d"] = a0["Wo"].ap()
    for nm in ["bq", "bo"]:
        t = al("b_" + nm, [128, 8], F32)
        P.dma("sp", t[:, :], a0b[nm].ap(), writes=[t.name], slot="lw")
        W[nm] = t
    ar.allocate()
    emit_big_attn(P, mem, W, {"pre": "l0a_", "alibi": True, "xres_in": xin.ap(),
                              "xres_out": x1.ap(), "dram": dram})
    P.barrier()
    ar.free()

    if n_stages >= 2:
        ar = nc.auto_arena()
        al = lambda n, s, d: ar.sbuf_tensor("s2_" + n, s, d)
        mem = alloc_ffn_buffers(al, sh)
        C = {**CG, **load_consts(P, nc, al, consts_np, ["prevmask", "rot"])}
        Wsb = {}
        for L in range(2):
            for nm in FFN_SMALLS:
                Wsb[(L, nm)] = al(f"L{L}_{nm}", list(fd[(L, nm)].shape), F32)
        ar.allocate()
        for L in range(2):
            W = {"consts": C}
            W["wqkv_d"] = fd[(L, "wqkv")].ap(); W["wo_d"] = fd[(L, "wo")].ap()
            W["w1_d"] = fd[(L, "w1p")].ap(); W["w2_d"] = fd[(L, "w2p")].ap()
            W["cs8q_d"] = dram["cs8q"]; W["sn8q_d"] = dram["sn8q"]
            W["csk_d"] = dram["csk"]; W["snk_d"] = dram["snk"]
            for nm in FFN_SMALLS:
                t = Wsb[(L, nm)]
                P.dma("sp", t[:, :], fd[(L, nm)].ap(), writes=[t.name], slot="lw")
                key = {"b1p": "b1sb", "b2": "b2sb", "lg": "lgsb", "lb": "lbsb"}.get(nm, nm)
                W[key] = t
            if L == 0:
                xi, xo = x1.ap(), xm.ap()
            else:
                for dt in range(DT):
                    P.dma("pool", sh["xbf"][dt][:, :], xm.ap()[dt * 128:(dt + 1) * 128, :],
                          writes=["xbf"], slot="xbf",
                          reads=[f"f_dxo{d}" for d in range(DT)])
                xi, xo = xm.ap(), x2.ap()
            emit_ffn_layer(P, mem, W, {"pre": "f_", "xres_in": xi, "xres_out": xo,
                                       "dram": dram, "last": (L == 1), "xorig": x1.ap()})
        P.barrier()
        ar.free()

    if n_stages >= 3:
        ar = nc.auto_arena()
        al = lambda n, s, d: ar.sbuf_tensor("s3_" + n, s, d)
        mem = alloc_attn_buffers(al, sh)
        W = {"consts": CG}
        W["setA"] = [al(f"wsetA{i}", [128, D], BF16) for i in range(DT)]
        W["setB"] = [al(f"wsetB{i}", [128, D], BF16) for i in range(DT)]
        W["wq_d"] = a1["WqT"].ap(); W["wk_d"] = a1["WkT"].ap()
        W["wv_d"] = a1["WvT"].ap(); W["wo_d"] = a1["WoT"].ap()
        for nm in ["bq", "bk", "bv", "bo"]:
            t = al("b_" + nm, [128, 8], F32)
            P.dma("sp", t[:, :], a1b[nm].ap(), writes=[t.name], slot="lw")
            W[nm] = t
        ar.allocate()
        emit_big_attn(P, mem, W, {"pre": "l1a_", "alibi": False, "xres_in": x2.ap(),
                                  "xres_out": x3.ap(), "dram": dram})
        P.barrier()
        ar.free()

    if n_stages >= 4:
        ar = nc.auto_arena()
        al = lambda n, s, d: ar.sbuf_tensor("s4_" + n, s, d)
        mem = alloc_l1ffn_buffers(al, sh)
        C = {**CG, **load_consts(P, nc, al, consts_np, ["ident", "identf"])}
        ar.allocate()
        W = {"consts": C, "wgs_d": qd["wgs"].ap(), "wus_d": qd["wus"].ap(),
             "wds_d": qd["wds"].ap()}
        emit_l1ffn(P, mem, W, {"pre": "q_", "xres_in": x3.ap(), "xres_out": xout.ap(),
                               "dram": dram})
        P.barrier()
        ar.free()

    P.emit(est)
    return nc, consts_np


_PROGS = {}


def _get_prog(name, builder):
    if name not in _PROGS:
        _PROGS[name] = builder()
    return _PROGS[name]


def mega_in_maps(g, consts_np, cores):
    def cvec(b):
        return np.ascontiguousarray(np.asarray(b, np.float32).reshape(8, 128).T)

    base = {}
    for k in CONST_LIST + ["vhl", "cs8q", "sn8q", "csk", "snk"]:
        base["c_" + k] = np.asarray(consts_np[k])
    base["a0_Wq"] = f32(g["L0_attn_Wq"]); base["a0_Wk"] = f32(g["L0_attn_Wk"])
    base["a0_Wv"] = f32(g["L0_attn_Wv"]); base["a0_Wo"] = f32(g["L0_attn_Wo"])
    base["a0_bq"] = cvec(g["L0_attn_bq"]); base["a0_bo"] = cvec(g["L0_attn_bo"])
    for L in range(2):
        hw = prep_ffn_host_weights(g["L0_ffn_wqkv"][L], g["L0_ffn_q_scale"][L],
                                   g["L0_ffn_k_scale"][L], g["L0_ffn_wo"][L],
                                   g["L0_ffn_ln_g"][L], g["L0_ffn_ln_b"][L],
                                   g["L0_ffn_w1"][L], g["L0_ffn_b1"][L],
                                   g["L0_ffn_w2"][L], g["L0_ffn_b2"][L])
        for nm, _ in FFN_WSHAPES:
            base[f"f{L}_{nm}"] = hw[nm]
    base["a1_WqT"] = np.ascontiguousarray(f32(g["L1_attn_wq"]).T)
    base["a1_WkT"] = np.ascontiguousarray(f32(g["L1_attn_wk"]).T)
    base["a1_WvT"] = np.ascontiguousarray(f32(g["L1_attn_wv"]).T)
    base["a1_WoT"] = np.ascontiguousarray(f32(g["L1_attn_wo"]).T)
    base["a1_bq"] = cvec(g["L1_attn_bq"]); base["a1_bk"] = cvec(g["L1_attn_bk"])
    base["a1_bv"] = cvec(g["L1_attn_bv"]); base["a1_bo"] = cvec(g["L1_attn_bo"])
    wg = f32(g["L1_ffn_w_gate"]); wu = f32(g["L1_ffn_w_up"])
    wd = f32(g["L1_ffn_w_down"])
    wdp = np.zeros((1024, 3072), np.float32)
    wdp[:, :2816] = wd
    xT = [np.ascontiguousarray(f32(g["L0_attn_x"][c]).T) for c in cores]
    return [dict(base, xT=xT[c],
                 q_wgs=np.ascontiguousarray(wg[:, c * 128:(c + 1) * 128]),
                 q_wus=np.ascontiguousarray(wu[:, c * 128:(c + 1) * 128]),
                 q_wds=np.ascontiguousarray(wdp[:, c * 384:(c + 1) * 384]))
            for c in cores]


def kernel(**inputs):
    g = {k: np.asarray(v) for k, v in inputs.items()}
    cores = list(range(g["L0_attn_x"].shape[0]))
    nc, consts_np = _get_prog("mega", build_mega_program)
    maps = mega_in_maps(g, consts_np, cores)
    res = run_bass_kernel_spmd(nc, maps, core_ids=cores)
    out = np.stack([np.asarray(r["xoutT"]).T for r in res.results], 0)
    return out.astype(np.float32)


# version 13
# speedup vs baseline: 1.1752x; 1.1752x over previous
import contextlib
import numpy as np
import ml_dtypes
import concourse.bass as bass
import concourse.mybir as mybir
from concourse.bass_utils import run_bass_kernel_spmd

F32 = mybir.dt.float32
BF16 = mybir.dt.bfloat16
I32 = mybir.dt.int32
AF = mybir.ActivationFunctionType
ALU = mybir.AluOpType
AX = mybir.AxisListType.X

NCORES = 8
S = 2048
D = 1024
NQ = 512
NT = S // NQ
DT = D // 128


ENGINES = ("pe", "act", "dve", "pool", "sp")


class Prog:

    def __init__(self, nc):
        self.nc = nc
        self.streams = {e: [] for e in ENGINES}
        self.counts = {s: 0 for s in ENGINES}
        self.semnames = list(ENGINES)
        self.lw = {}
        self.readers = {}
        self.wm = {e: {} for e in ENGINES}

    def _sem(self, name):
        if name not in self.counts:
            self.counts[name] = 0
            self.semnames.append(name)
        return name

    def _waits(self, eng, reads, writes):
        need = {}

        def add(s, v):
            if s.startswith("D_"):
                v = self.counts[s]
            if need.get(s, 0) < v:
                need[s] = v

        for t in reads:
            if t in self.lw:
                add(*self.lw[t])
        for t in writes:
            rd = self.readers.get(t)
            if rd:
                for s, v in rd.items():
                    add(s, v)
            elif t in self.lw:
                add(*self.lw[t])
        w = self.wm[eng]
        out = []
        for s, v in need.items():
            if w.get(s, 0) < v:
                out.append((s, v))
                w[s] = v
        return out

    def _book(self, sem, val, reads, writes):
        for t in writes:
            self.lw[t] = (sem, val)
            self.readers.pop(t, None)
        for t in reads:
            d = self.readers.setdefault(t, {})
            if d.get(sem, 0) < val:
                d[sem] = val

    def op(self, eng, fn, reads=(), writes=()):
        waits = self._waits(eng, reads, writes)
        self.counts[eng] += 1
        val = self.counts[eng]
        self.streams[eng].append((fn, waits, eng, val, 1))
        self._book(eng, val, reads, writes)

    def dma(self, eng, out, in_, reads=(), writes=(), slot=None, **kw):
        assert writes, "dma needs a write token"
        sem = self._sem("D_" + (slot or writes[0]))
        waits = self._waits(eng, reads, writes)
        self.counts[sem] += 16
        val = self.counts[sem]

        def fn(e, out=out, in_=in_, kw=kw):
            return e.dma_start(out=out, in_=in_, **kw)

        self.streams[eng].append((fn, waits, sem, val, 16))
        self._book(sem, val, reads, writes)

    def barrier(self, engines=ENGINES):
        for eng in engines:
            w = self.wm[eng]
            waits = []
            for s in self.semnames:
                c = self.counts[s]
                if c > 0 and w.get(s, 0) < c:
                    waits.append((s, c))
                    w[s] = c
            if waits:
                self.streams[eng].append((None, waits, None, 0, 0))

    def emit(self, est):
        semh = {s: est.enter_context(self.nc.semaphore(s)) for s in self.semnames
                if self.counts[s] > 0 or s in ENGINES}
        block = est.enter_context(self.nc.Block())

        def run(eng_name):
            def body(e):
                for fn, waits, sem, val, inc in self.streams[eng_name]:
                    for (ws, wv) in waits:
                        e.wait_ge(semh[ws], wv)
                    if fn is not None:
                        inst = fn(e)
                        if sem is not None:
                            inst.then_inc(semh[sem], inc)
            return body

        block.tensor(run("pe"))
        block.scalar(run("act"))
        block.vector(run("dve"))
        block.gpsimd(run("pool"))
        block.sync(run("sp"))



def f32(a):
    return np.ascontiguousarray(np.asarray(a), dtype=np.float32)


def to_bf16(a):
    return np.ascontiguousarray(np.asarray(a, dtype=np.float32).astype(ml_dtypes.bfloat16))


SLOPES = 2.0 ** (-0.5 * np.arange(1, 17))
ALIBI_SLACK = 220.0
DCUT = [int(np.ceil(ALIBI_SLACK / s)) for s in SLOPES]
ALIBI_MARGIN = 4.0
PREMASK_HEADS = [h for h in range(16) if SLOPES[h] * 511 - ALIBI_MARGIN > 78.0]


def build_attn_consts():
    c = {}
    c["ones128c"] = to_bf16(np.ones((128, 1)))
    c["ones1r"] = f32(np.ones((1, 128)))
    bd = np.zeros((128, 128), np.float32)
    bd[:64, :64] = 1.0
    bd[64:, 64:] = 1.0
    c["bd64"] = to_bf16(bd)
    e2 = np.zeros((2, 128), np.float32)
    e2[0, :64] = 1.0
    e2[1, 64:] = 1.0
    c["e2pair"] = e2
    m = np.zeros((128, 4 * 512), np.float32)
    mneg = np.zeros((128, 4 * 512), np.float32)
    for j in range(4):
        kk = 128 * j + np.arange(128)[:, None]
        qq = np.arange(512)[None, :]
        alive = (kk <= qq)
        m[:, j * 512:(j + 1) * 512] = alive
        mneg[:, j * 512:(j + 1) * 512] = np.where(alive, 0.0, -1e9)
    c["diagmask"] = to_bf16(m)
    c["diagneg"] = to_bf16(mneg)
    slb = np.zeros((128, NT * 16 * 16), np.float32)
    p = np.arange(128)[:, None]
    for qc in range(NT):
        for h in range(16):
            for t in range(16):
                col = (qc * 16 + h) * 16 + t
                slb[:, col:col + 1] = SLOPES[h] * (128 * t + p - (512 * qc + 256))
    c["slb"] = f32(slb)
    v = -8.0 * SLOPES[:, None] * (np.arange(512)[None, :] - 256.0)
    hi = v.astype(ml_dtypes.bfloat16).astype(np.float64)
    lo = (v - hi).astype(ml_dtypes.bfloat16).astype(np.float64)
    vhl = np.zeros((2, 16 * 512), np.float32)
    vhl[0] = hi.reshape(-1)
    vhl[1] = lo.reshape(-1)
    c["vhl"] = to_bf16(vhl)
    return c



def emit_big_attn(P, mem, W, opts):
    pre = opts["pre"]
    alibi = opts["alibi"]
    xres_in = opts["xres_in"]
    xres_out = opts["xres_out"]
    dram = opts["dram"]

    xbf = mem["xbf"]
    kt = mem["kt"]
    vaug = mem["vaug"]
    qt = mem["qt"]
    kaug = mem["kaug"]
    qaug = mem["qaug"]
    ptile = mem["ptile"]
    attO = mem["attO"]
    xc32 = mem["xc32"]
    sq = mem["sq"]
    xcbf = mem["xcbf"]
    sm = mem["smalls"]
    ssum = mem["ssum"]
    r16 = mem["r16"]
    r2 = mem["r2"]
    brep = mem["brep"]
    biasm = mem["biasm"]

    rrep = mem["rrep"]
    mrrep = mem["mrrep"]
    ps = mem["psum"]
    C = W["consts"]

    wA, wB = W["setA"], W["setB"]
    bq, bk, bv, bo = W.get("bq"), W.get("bk"), W.get("bv"), W.get("bo")
    for dt in range(DT):
        P.dma("pool", wA[dt][:, :], W["wk_d"][dt * 128:(dt + 1) * 128, :], writes=["wsetA"])
        P.dma("pool", wB[dt][:, :], W["wv_d"][dt * 128:(dt + 1) * 128, :], writes=["wsetB"])
    wk = wA
    wv = wB

    def tok(n):
        return pre + n

    VW = 16 * 65

    def mseps(e):
        return e.memset(mem["eps11"][0:1, 0:1], 1e-5)
    P.op("dve", mseps, writes=["eps11"])
    if alibi:
        for i in range(2):
            def msk1(e, i=i):
                return e.memset(kaug[i][64:66, :], 1.0)
            P.op("dve", msk1, writes=[tok(f"ka{i}")])

    for p8 in range(8):
        for ch in range(NT):
            kb_ = ("a", "e")[(p8 * NT + ch) % 2]

            def mmk(e, p8=p8, ch=ch, kb_=kb_):
                last = None
                for dt in range(DT):
                    last = e.matmul(ps[kb_][:, :NQ], wk[dt][:, p8 * 128:(p8 + 1) * 128],
                                    xbf[dt][:, ch * NQ:(ch + 1) * NQ],
                                    start=(dt == 0), stop=(dt == DT - 1))
                return last
            P.op("pe", mmk, reads=["wsetA", "xbf"], writes=["ps_" + kb_])

            def evk(e, p8=p8, ch=ch, kb_=kb_):
                if bk is not None:
                    return e.activation(kt[p8][:, ch * NQ:(ch + 1) * NQ], ps[kb_][:, :NQ],
                                        AF.Identity, bias=bk[:, p8:p8 + 1])
                return e.copy(kt[p8][:, ch * NQ:(ch + 1) * NQ], ps[kb_][:, :NQ])
            P.op("act", evk, reads=["ps_" + kb_] + ([bk.name] if bk is not None else []),
                 writes=[tok(f"kt{p8}")])

    for tt in range(16):
        def msv(e, tt=tt):
            return e.memset(vaug[tt][:, :], 0.0)
        P.op("dve", msv, writes=[tok(f"va{tt}")])

        def onecol(e, tt=tt):
            ap = bass.AP(vaug[tt], 64, [[VW, 128], [65, 16], [1, 1]])
            return e.memset(ap, 1.0)
        P.op("dve", onecol, reads=[tok(f"va{tt}")], writes=[tok(f"va{tt}")])
    for tt in range(16):
        for half in range(2):
            vb_ = ("a", "e")[(tt * 2 + half) % 2]

            def mmv(e, tt=tt, half=half, vb_=vb_):
                last = None
                for dt in range(DT):
                    last = e.matmul(ps[vb_][:, :NQ], xbf[dt][:, tt * 128:(tt + 1) * 128],
                                    wv[dt][:, half * 512:(half + 1) * 512],
                                    start=(dt == 0), stop=(dt == DT - 1))
                return last
            P.op("pe", mmv, reads=["wsetB", "xbf"], writes=["ps_" + vb_])

            def evv(e, tt=tt, half=half, vb_=vb_):
                o = bass.AP(vaug[tt], (half * 8) * 65, [[VW, 128], [65, 8], [1, 64]])
                i = ps[vb_].ap().rearrange("p (h c) -> p h c", c=64)[:, :, :]
                return e.copy(o, i)
            P.op("act", evv, reads=["ps_" + vb_], writes=[tok(f"va{tt}")])

    for p8 in range(8):
        for ch in range(NT):
            def sqk(e, p8=p8, ch=ch):
                return e.activation(sq[:, :NQ], kt[p8][:, ch * NQ:(ch + 1) * NQ], AF.Square)
            P.op("act", sqk, reads=[tok(f"kt{p8}")], writes=[tok("sq")])

            def mmn(e):
                return e.matmul(ps["b"][:, :NQ], C["bd64"][:, :], sq[:, :NQ], start=True, stop=True)
            P.op("pe", mmn, reads=[tok("sq"), "c"], writes=["ps_b"])

            def rmx(e, p8=p8, ch=ch):
                return e.tensor_reduce(sm[:, p8 * NT + ch:p8 * NT + ch + 1], ps["b"][:, :NQ],
                                       axis=AX, op=ALU.max)
            P.op("dve", rmx, reads=["ps_b"], writes=[tok("kn")])

    def rk2(e):
        return e.tensor_reduce(sm[:, 32:33], sm[:, 0:32], axis=AX, op=ALU.max)
    P.op("dve", rk2, reads=[tok("kn")], writes=[tok("kn1")])
    P.dma("sp", bass.AP(dram["bounce"].tensor, 0, [[1, 128], [1, 1]]), sm[:, 32:33],
          reads=[tok("kn1")], writes=[tok("dbK")], slot="dbK")
    P.dma("sp", mem["kbrow"][0:1, :], bass.AP(dram["bounce"].tensor, 0, [[128, 1], [1, 128]]),
          reads=[tok("dbK")], writes=[tok("knrow")], slot="knrow")

    def rk3(e):
        return e.tensor_reduce(sm[0:1, 33:34], mem["kbrow"][0:1, :], axis=AX, op=ALU.max)
    P.op("dve", rk3, reads=[tok("knrow")], writes=[tok("kmax")])

    dbg = opts.get("debug", {})
    if "kt0" in dbg:
        P.dma("act", dbg["kt0"], kt[0][:, :], reads=[tok("kt0")], writes=["dbg_kt0"])
    if "vaug0" in dbg:
        P.dma("act", dbg["vaug0"], vaug[0][:, :], reads=[tok("va0")], writes=["dbg_vaug0"])
    for dt in range(DT):
        P.dma("pool", wA[dt][:, :], W["wq_d"][dt * 128:(dt + 1) * 128, :], writes=["wsetA"])
        P.dma("pool", wB[dt][:, :], W["wo_d"][dt * 128:(dt + 1) * 128, :], writes=["wsetB"])
    wq = wA
    wo = wB

    def qphaseA(qc):
        qlo = qc * NQ

        for p8 in range(8):
            qb_ = ("a", "e")[p8 % 2]

            def mmq(e, p8=p8, qc=qc, qb_=qb_):
                last = None
                for dt in range(DT):
                    last = e.matmul(ps[qb_][:, :NQ], wq[dt][:, p8 * 128:(p8 + 1) * 128],
                                    xbf[dt][:, qc * NQ:(qc + 1) * NQ],
                                    start=(dt == 0), stop=(dt == DT - 1))
                return last
            P.op("pe", mmq, reads=["wsetA", "xbf"], writes=["ps_" + qb_])

            def evq(e, p8=p8, qb_=qb_):
                if bq is not None:
                    return e.activation(qt[p8][:, :], ps[qb_][:, :NQ], AF.Identity, bias=bq[:, p8:p8 + 1])
                return e.copy(qt[p8][:, :], ps[qb_][:, :NQ])
            P.op("act", evq, reads=["ps_" + qb_] + ([bq.name] if bq is not None else []),
                 writes=[tok(f"qt{p8}")])

        for p8 in range(8):
            def sqq(e, p8=p8):
                return e.activation(sq[:, :NQ], qt[p8][:, :], AF.Square)
            P.op("act", sqq, reads=[tok(f"qt{p8}")], writes=[tok("sq")])

            def mmn2(e):
                return e.matmul(ps["b"][:, :NQ], C["bd64"][:, :], sq[:, :NQ], start=True, stop=True)
            P.op("pe", mmn2, reads=[tok("sq"), "c"], writes=["ps_b"])

            def rmq(e, p8=p8):
                return e.tensor_reduce(sm[:, 34 + p8:35 + p8], ps["b"][:, :NQ], axis=AX, op=ALU.max)
            P.op("dve", rmq, reads=["ps_b"], writes=[tok("qn")])

        def rq2(e):
            return e.tensor_reduce(sm[:, 42:43], sm[:, 34:42], axis=AX, op=ALU.max)
        P.op("dve", rq2, reads=[tok("qn")], writes=[tok("qn1")])

    def qphaseB(qc):
        qlo = qc * NQ
        P.dma("sp", bass.AP(dram["bounce"].tensor, 128, [[1, 128], [1, 1]]), sm[:, 42:43],
              reads=[tok("qn1")], writes=[tok("dbQ")], slot="dbQ")
        P.dma("sp", mem["qbrow"][0:1, :], bass.AP(dram["bounce"].tensor, 128, [[128, 1], [1, 128]]),
              reads=[tok("dbQ")], writes=[tok("qnrow")], slot="qnrow")

        def rq3(e):
            return e.tensor_reduce(sm[0:1, 43:44], mem["qbrow"][0:1, :], axis=AX, op=ALU.max)
        P.op("dve", rq3, reads=[tok("qnrow")], writes=[tok("qmax")])

        def bprod(e):
            return e.tensor_tensor(sm[0:1, 44:45], sm[0:1, 43:44], sm[0:1, 33:34], ALU.mult)
        P.op("dve", bprod, reads=[tok("qmax"), tok("kmax")], writes=[tok("bp")])

        def bln(e):
            return e.activation(sm[0:1, 44:45], sm[0:1, 44:45], AF.Ln)
        P.op("act", bln, reads=[tok("bp")], writes=[tok("bp")])
        margin = ALIBI_MARGIN if alibi else 4.0

        def bexp(e):
            return e.activation(sm[0:1, 44:45], sm[0:1, 44:45], AF.Exp, scale=0.5)
        P.op("act", bexp, reads=[tok("bp")], writes=[tok("bp")])

        def bneg(e, margin=margin):
            return e.tensor_scalar(sm[0:1, 45:46], sm[0:1, 44:45], -1.0 / 64.0, -margin,
                                   ALU.mult, ALU.add)
        P.op("dve", bneg, reads=[tok("bp")], writes=[tok("nb")])
        P.dma("sp", bass.AP(dram["bounce"].tensor, 256, [[1, 1], [1, 1]]), sm[0:1, 45:46],
              reads=[tok("nb")], writes=[tok("dbB")], slot="dbB")
        P.dma("sp", brep[:, 0:1], bass.AP(dram["bounce"].tensor, 256, [[0, 128], [1, 1]]),
              reads=[tok("dbB")], writes=[tok("brep")], slot="brep")
        if alibi:
            def mkbias(e, qc=qc):
                return e.tensor_scalar(biasm[:, :], W["slb"][:, qc * 256:(qc + 1) * 256],
                                       brep[:, 0:1], None, ALU.add)
            P.op("dve", mkbias, reads=["c", tok("brep")], writes=[tok("biasm")])

        if qc == 0 and "qt0" in opts.get("debug", {}):
            P.dma("act", opts["debug"]["qt0"], qt[0][:, :], reads=[tok("qt0")], writes=["dbg_qt0"])
            P.dma("act", opts["debug"]["brep"], brep[:, 0:1], reads=[tok("brep")], writes=["dbg_brep"])

    qphaseA(0)
    qphaseB(0)
    for qc in range(NT):
        qlo = qc * NQ
        qtile = qc * 4
        for h in range(16):
            p8 = h // 2
            base = 64 * (h % 2)
            par = h % 2
            if alibi:
                kb, qb = kaug[par], qaug[par]

                def cpk(e, p8=p8, base=base, kb=kb):
                    return e.tensor_copy(kb[0:64, :], kt[p8][base:base + 64, :])
                P.op("dve", cpk, reads=[tok(f"kt{p8}")], writes=[tok(f"ka{par}")])

                def cpq(e, p8=p8, base=base, qb=qb):
                    return e.tensor_copy(qb[0:64, :], qt[p8][base:base + 64, :])
                P.op("dve", cpq, reads=[tok(f"qt{p8}")], writes=[tok(f"qa{par}")])

                P.dma("sp", qb[64:66, :], W["vhl_d"][0:2, h * 512:(h + 1) * 512],
                      writes=[tok(f"qa{par}")], slot=f"qa{par}")
                lo_t = max(0, (qlo - DCUT[h]) // 128)
            else:
                lo_t = 0
            ktiles = list(range(lo_t, qtile + 4))
            nk = len(ktiles)
            premask = alibi and (h in PREMASK_HEADS)

            def emit_score(ki):
                t = ktiles[ki]
                pband = "c" if (ki % 2 == 0) else "d"
                if alibi:
                    def mms(e, t=t, par=par, pband=pband):
                        return e.matmul(ps[pband][:, :NQ], kaug[par][0:66, t * 128:(t + 1) * 128],
                                        qaug[par][0:66, :], start=True, stop=True)
                    P.op("pe", mms, reads=[tok(f"ka{par}"), tok(f"qa{par}")], writes=["ps_" + pband])
                else:
                    def mms(e, t=t, p8=p8, base=base, pband=pband):
                        return e.matmul(ps[pband][:, :NQ],
                                        kt[p8][base:base + 64, t * 128:(t + 1) * 128],
                                        qt[p8][base:base + 64, :], start=True, stop=True)
                    P.op("pe", mms, reads=[tok(f"kt{p8}"), tok(f"qt{p8}")], writes=["ps_" + pband])
                dj = t - qtile
                if premask and dj >= 0:
                    def pmk(e, dj=dj, pband=pband):
                        return e.scalar_tensor_tensor(ps[pband][:, :NQ], ps[pband][:, :NQ], 1.0,
                                                      C["diagneg"][:, dj * 512:(dj + 1) * 512],
                                                      ALU.mult, ALU.add)
                    P.op("dve", pmk, reads=["ps_" + pband, "c"], writes=["ps_" + pband])
                pt = ptile[ki % 2]
                if alibi:
                    def evx(e, t=t, h=h, pband=pband, pt=pt):
                        return e.activation(pt[:, :], ps[pband][:, :NQ], AF.Exp,
                                            bias=biasm[:, h * 16 + t:h * 16 + t + 1], scale=0.125)
                    P.op("act", evx, reads=["ps_" + pband, tok("biasm")], writes=[tok(f"pt{ki%2}")])
                else:
                    def evx(e, pband=pband, pt=pt):
                        return e.activation(pt[:, :], ps[pband][:, :NQ], AF.Exp,
                                            bias=brep[:, 0:1], scale=0.125)
                    P.op("act", evx, reads=["ps_" + pband, tok("brep")], writes=[tok(f"pt{ki%2}")])
                if dj >= 0 and not premask:
                    def msk(e, dj=dj, pt=pt):
                        return e.tensor_tensor(pt[:, :], pt[:, :],
                                               C["diagmask"][:, dj * 512:(dj + 1) * 512], ALU.mult)
                    P.op("dve", msk, reads=[tok(f"pt{ki%2}"), "c"], writes=[tok(f"pt{ki%2}")])

            def emit_pv(ki):
                t = ktiles[ki]
                def mpv(e, t=t, h=h, ki=ki, nk=nk):
                    return e.matmul(ps["e"][0:65, :NQ], vaug[t][:, h * 65:h * 65 + 65],
                                    ptile[ki % 2][:, :], start=(ki == 0), stop=(ki == nk - 1))
                P.op("pe", mpv, reads=[tok(f"va{t}"), tok(f"pt{ki%2}")], writes=["ps_e"])

            emit_score(0)
            for ki in range(1, nk):
                emit_score(ki)
                emit_pv(ki - 1)
            emit_pv(nk - 1)

            def evo(e, p8=p8, base=base):
                return e.copy(attO[p8][base:base + 64, :], ps["e"][0:64, :NQ])
            P.op("act", evo, reads=["ps_e"], writes=[tok(f"at{p8}")])

            sb = ssum[h % 2]
            def evs(e, sb=sb):
                return e.copy(sb[0:1, :], ps["e"][64:65, :NQ])
            P.op("act", evs, reads=["ps_e"], writes=[tok(f"ss{h%2}")])
            P.dma("sp", bass.AP(dram["sums"].tensor, h * 512, [[512, 1], [1, 512]]), sb[0:1, :],
                  reads=[tok(f"ss{h%2}")], writes=[tok("dS")], slot="dS")

        if qc + 1 < NT:
            qphaseA(qc + 1)
        dbg = opts.get("debug", {})
        if qc == 0 and "attO0" in dbg:
            P.dma("act", dbg["attO0"], attO[0][:, :], reads=[tok("at0")], writes=["dbg_attO0"])
        P.dma("sp", r16[:, :], bass.AP(dram["sums"].tensor, 0, [[512, 16], [1, 512]]),
              reads=[tok("dS")], writes=[tok("r16")], slot="r16")

        if qc == 0 and "rawsums" in opts.get("debug", {}):
            P.dma("act", opts["debug"]["rawsums"], r16[:, :], reads=[tok("r16")], writes=["dbg_rawsums"])

        def lns(e):
            return e.activation(r16[:, :], r16[:, :], AF.Ln)
        P.op("act", lns, reads=[tok("r16")], writes=[tok("r16")])

        def exns(e):
            return e.activation(r16[:, :], r16[:, :], AF.Exp, scale=-1.0)
        P.op("act", exns, reads=[tok("r16")], writes=[tok("r16")])
        P.dma("sp", dram["r16d"], r16[:, :], reads=[tok("r16")], writes=[tok("dR")], slot="dR")
        if qc == 0 and "r16" in opts.get("debug", {}):
            P.dma("act", opts["debug"]["r16"], r16[:, :], reads=[tok("r16")], writes=["dbg_r16"])
        for p8 in range(8):
            rp = r2[p8 % 2]
            P.dma("sp", rp[:, :], bass.AP(dram["r16d"].tensor, 0, [[512, 2], [1024, 8], [1, 512]])[:, p8, :],
                  reads=[tok("dR")], writes=[tok(f"r2{p8%2}")], slot=f"r2{p8%2}")
            def mrep(e, p8=p8, rp=rp):
                return e.matmul(ps["b"][:, :NQ], C["e2pair"][0:2, :], rp[0:2, :],
                                start=True, stop=True)
            P.op("pe", mrep, reads=[tok(f"r2{p8%2}"), "c"], writes=["ps_b"])
            def nrm(e, p8=p8):
                return e.tensor_tensor(attO[p8][:, :], attO[p8][:, :], ps["b"][:, :NQ], ALU.mult)
            P.op("dve", nrm, reads=["ps_b", tok(f"at{p8}")], writes=[tok(f"at{p8}")])
            if bv is not None:
                def nrm1(e, p8=p8):
                    return e.activation(attO[p8][:, :], attO[p8][:, :], AF.Identity,
                                        bias=bv[:, p8:p8 + 1])
                P.op("act", nrm1, reads=[tok(f"at{p8}"), bv.name], writes=[tok(f"at{p8}")])

        if qc + 1 < NT:
            qphaseB(qc + 1)
        for dt in range(DT):
            P.dma(("sp", "act", "pool")[dt % 3], xc32[dt][:, :],
                  xres_in[dt * 128:(dt + 1) * 128, qlo:qlo + NQ],
                  writes=[tok(f"xl{dt}")], slot=f"xl{dt}")
            ob_ = ("a", "e")[dt % 2]

            def mmo(e, dt=dt, ob_=ob_):
                last = None
                for ot in range(DT):
                    last = e.matmul(ps[ob_][:, :NQ], wo[ot][:, dt * 128:(dt + 1) * 128],
                                    attO[ot][:, :], start=(ot == 0), stop=(ot == DT - 1))
                return last
            P.op("pe", mmo, reads=["wsetB"] + [tok(f"at{o}") for o in range(DT)],
                 writes=["ps_" + ob_])

            if bo is not None:
                def radd(e, dt=dt, ob_=ob_):
                    return e.scalar_tensor_tensor(xc32[dt][:, :], ps[ob_][:, :NQ],
                                                  bo[:, dt:dt + 1], xc32[dt][:, :],
                                                  ALU.add, ALU.add)
            else:
                def radd(e, dt=dt, ob_=ob_):
                    return e.tensor_tensor(xc32[dt][:, :], ps[ob_][:, :NQ], xc32[dt][:, :], ALU.add)
            P.op("dve", radd,
                 reads=[tok(f"xl{dt}"), "ps_" + ob_] + ([bo.name] if bo is not None else []),
                 writes=[tok(f"xl{dt}")])

        emit_postln(P, mem, C, tok, xc32, xbf, qlo, xres_out, pre)


def emit_postln(P, mem, C, tok, xc32, xbf, qlo, xres_out, pre):
    sq = mem["sq"]
    xcbf = mem["xcbf"]
    statm = mem["statm"]
    stats = mem["stats"]
    statm2 = mem["rrep"]
    rrep = mem["rrep"]
    mrrep = mem["mrrep"]
    ps = mem["psum"]

    for dt in range(DT):
        def cbx(e, dt=dt):
            return e.tensor_copy(xcbf[:, :], xc32[dt][:, :])
        P.op("dve", cbx, reads=[tok(f"xl{dt}")], writes=[tok("xcbf")])

        def mmsx(e, dt=dt):
            return e.matmul(ps["f"][0:1, :NQ], C["ones128c"][:, 0:1], xcbf[:, :],
                            start=(dt == 0), stop=(dt == DT - 1))
        P.op("pe", mmsx, reads=[tok("xcbf"), "c"], writes=["ps_f"])

        def sqx(e, dt=dt):
            return e.activation(sq[:, :NQ], xc32[dt][:, :], AF.Square)
        P.op("act", sqx, reads=[tok(f"xl{dt}")], writes=[tok("sq")])

        def mmsq(e, dt=dt):
            return e.matmul(ps["g"][0:1, :NQ], C["ones128c"][:, 0:1], sq[:, :NQ],
                            start=(dt == 0), stop=(dt == DT - 1))
        P.op("pe", mmsq, reads=[tok("sq"), "c"], writes=["ps_g"])

    def st1(e):
        return e.tensor_scalar(statm[0:1, :], ps["f"][0:1, :NQ], 1.0 / D, None, ALU.mult)
    P.op("dve", st1, reads=["ps_f"], writes=[tok("stm")])

    def st2(e):
        return e.tensor_scalar(stats[0:1, :], ps["g"][0:1, :NQ], 1.0 / D, None, ALU.mult)
    P.op("dve", st2, reads=["ps_g"], writes=[tok("sts")])

    def st3(e):
        return e.tensor_tensor(statm2[0:1, :], statm[0:1, :], statm[0:1, :], ALU.mult)
    P.op("dve", st3, reads=[tok("stm")], writes=[tok("rrep")])

    def st4(e):
        return e.tensor_tensor(stats[0:1, :], stats[0:1, :], statm2[0:1, :], ALU.subtract)
    P.op("dve", st4, reads=[tok("sts"), tok("rrep")], writes=[tok("sts")])

    def st5(e):
        return e.activation(stats[0:1, :], stats[0:1, :], AF.Ln, bias=mem["eps11"][0:1, 0:1])
    P.op("act", st5, reads=[tok("sts"), "eps11"], writes=[tok("sts")])

    def st6(e):
        return e.activation(stats[0:1, :], stats[0:1, :], AF.Exp, scale=-0.5)
    P.op("act", st6, reads=[tok("sts")], writes=[tok("sts")])

    def st7(e):
        return e.tensor_tensor(statm[0:1, :], statm[0:1, :], stats[0:1, :], ALU.mult)
    P.op("dve", st7, reads=[tok("stm"), tok("sts")], writes=[tok("stm")])

    def mr1(e):
        return e.matmul(ps["f"][:, :NQ], C["ones1r"][0:1, :], stats[0:1, :], start=True, stop=True)
    P.op("pe", mr1, reads=[tok("sts"), "c"], writes=["ps_f"])

    def mr2(e):
        return e.matmul(ps["g"][:, :NQ], C["ones1r"][0:1, :], statm[0:1, :], start=True, stop=True)
    P.op("pe", mr2, reads=[tok("stm"), "c"], writes=["ps_g"])

    def evr(e):
        return e.copy(rrep[:, :], ps["f"][:, :NQ])
    P.op("act", evr, reads=["ps_f"], writes=[tok("rrep")])

    def evm(e):
        return e.copy(mrrep[:, :], ps["g"][:, :NQ])
    P.op("act", evm, reads=["ps_g"], writes=[tok("mrrep")])

    for dt in range(DT):
        def ap1(e, dt=dt):
            return e.tensor_tensor(xc32[dt][:, :], xc32[dt][:, :], rrep[:, :], ALU.mult)
        P.op("dve", ap1, reads=[tok(f"xl{dt}"), tok("rrep")], writes=[tok(f"xl{dt}")])

        def ap2(e, dt=dt):
            return e.tensor_tensor(xc32[dt][:, :], xc32[dt][:, :], mrrep[:, :], ALU.subtract)
        P.op("dve", ap2, reads=[tok(f"xl{dt}"), tok("mrrep")], writes=[tok(f"xl{dt}")])
        P.dma(("sp", "act", "pool")[dt % 3], xres_out[dt * 128:(dt + 1) * 128, qlo:qlo + NQ],
              xc32[dt][:, :], reads=[tok(f"xl{dt}")], writes=[pre + f"dxo{dt}"],
              slot=f"dxo{dt}")

        if xbf is not None:
            def cb2(e, dt=dt):
                return e.tensor_copy(xbf[dt][:, qlo:qlo + NQ], xc32[dt][:, :])
            P.op("dve", cb2, reads=[tok(f"xl{dt}")], writes=["xbf"])



def make_shared(nc, est):
    E = est.enter_context
    sh = {}
    sh["xbf"] = [E(nc.sbuf_tensor(f"xbf{i}", [128, S], BF16)) for i in range(DT)]
    sh["psum"] = {k: E(nc.psum_tensor(f"ps_{k}", [128, 512], F32)) for k in "abcdefgh"}
    sh["psumh"] = sh["psum"]["h"].bitcast(BF16)
    sh["eps11"] = E(nc.sbuf_tensor("eps11", [1, 1], F32))
    sh["eps18"] = E(nc.sbuf_tensor("eps18", [128, 1], F32))
    return sh


def alloc_attn_buffers(alloc, sh):
    mem = {}
    mem["xbf"] = sh["xbf"]
    mem["kt"] = [alloc(f"kt{i}", [128, S], BF16) for i in range(8)]
    mem["vaug"] = [alloc(f"vaug{i}", [128, 16 * 65], BF16) for i in range(16)]
    mem["qt"] = [alloc(f"qt{i}", [128, NQ], BF16) for i in range(8)]
    mem["kaug"] = [alloc(f"kaug{i}", [128, S], BF16) for i in range(2)]
    mem["qaug"] = [alloc(f"qaug{i}", [128, NQ], BF16) for i in range(2)]
    mem["ptile"] = [alloc(f"ptile{i}", [128, NQ], BF16) for i in range(2)]
    mem["attO"] = [alloc(f"attO{i}", [128, NQ], BF16) for i in range(DT)]
    mem["xc32"] = [alloc(f"xc32_{i}", [128, NQ], F32) for i in range(DT)]
    mem["sq"] = alloc("sq", [128, NQ], BF16)
    mem["xcbf"] = alloc("xcbf", [128, NQ], BF16)
    mem["smalls"] = alloc("smalls", [128, 48], F32)
    mem["r16"] = alloc("r16", [16, 512], F32)
    mem["r2"] = [alloc(f"r2_{i}", [2, 512], F32) for i in range(2)]
    mem["ssum"] = [t for t in mem["r2"]]
    mem["brep"] = alloc("brep", [128, 1], F32)
    mem["biasm"] = alloc("biasm", [128, 256], F32)
    mem["statm"] = alloc("statm", [1, NQ], F32)
    mem["stats"] = alloc("stats", [1, NQ], F32)
    mem["eps11"] = sh["eps11"]
    mem["kbrow"] = alloc("kbrow", [1, 128], F32)
    mem["qbrow"] = alloc("qbrow", [1, 128], F32)
    mem["rrep"] = alloc("rrep", [128, NQ], F32)
    mem["mrrep"] = alloc("mrrep", [128, NQ], F32)
    mem["psum"] = sh["psum"]
    return mem


def load_consts(P, nc, alloc, consts_np, which, queue="sp"):
    out = {}
    for k in which:
        a = consts_np[k]
        dt = BF16 if a.dtype == ml_dtypes.bfloat16 else F32
        h = nc.dram_tensor("c_" + k, list(a.shape), dt, kind="ExternalInput")
        t = alloc("sc_" + k, list(a.shape), dt)
        P.dma(queue, t[:, :], h.ap(), writes=["c"], slot="c")
        out[k] = t
    return out



def build_ffn_consts():
    c = {}
    DH, WIN = 64, 512
    half = np.arange(0, DH, 2, dtype=np.float64)
    inv_freq = 1.0 / 10000.0 ** (half / DH)
    t = np.arange(S, dtype=np.float64)
    freqs = t[:, None] * inv_freq[None, :]
    freqs2 = np.concatenate([freqs, freqs], 1)
    base = (half + 0.4 * DH) / (1.4 * DH)
    power = (t - S // 2) / float(WIN // 2)
    scale = base[None, :] ** power[:, None]
    scale2 = np.concatenate([scale, scale], 1)
    cos = np.cos(freqs2); sin = np.sin(freqs2)
    def two(v):
        return np.concatenate([v.T, v.T], 0)
    c["cs8q"] = to_bf16(two(8.0 * cos * scale2))
    c["sn8q"] = to_bf16(two(8.0 * sin * scale2))
    c["csk"] = to_bf16(two(cos / scale2))
    c["snk"] = to_bf16(two(sin / scale2))
    R = np.zeros((128, 128), np.float32)
    for m in range(128):
        b = (m // 64) * 64
        dm = m % 64
        if dm < 32:
            R[b + dm + 32, m] = -1.0
        else:
            R[b + dm - 32, m] = 1.0
    c["rot"] = to_bf16(R)
    m = np.zeros((128, 4 * 512), np.float32)
    for djj in range(4):
        kk = 128 * djj + np.arange(128)[:, None]
        jj = np.arange(512)[None, :]
        m[:, djj * 512:(djj + 1) * 512] = (kk >= jj)
    c["prevmask"] = to_bf16(m)
    return c



def emit_ffn_layer(P, mem, W, opts):
    pre = opts["pre"]
    xres_in = opts["xres_in"]
    xres_out = opts["xres_out"]
    dram = opts["dram"]
    C = W["consts"]
    ps = mem["psum"]
    xbf = mem["xbf"]
    xc32 = mem["xc32"]
    sq = mem["sq"]
    r16 = mem["r16"]
    r2 = mem["r2"]
    ssum = mem["ssum"]
    sm = mem["smalls"]
    brep = mem["brep"]
    attO = mem["attO"]
    ptile = mem["ptile"]
    hbf0 = mem["hbf0"]
    wqkvsb = mem["wqkvsb"]
    k2 = mem["k2"]
    q2 = mem["q2"]
    qraw = mem["qraw"]
    vaug = mem["vaug"]
    h2c = mem["h2c"]
    prevcol = mem["prevcol"]
    hln = mem["hln"]
    pT = mem["pT"]
    w1sb = mem["w1sb"]
    w2sb = mem["w2sb"]
    wosb = mem["wosb"]
    xo32 = mem["xo32"]
    xcbf = mem["xcbf"]
    VW8 = 8 * 65

    def tok(n):
        return pre + n

    for dt in range(DT):
        P.dma("pool", wqkvsb[dt][:, :], W["wqkv_d"][dt * 128:(dt + 1) * 128, 0:1024],
              writes=[tok(f"wqkv{dt}")], slot="wqkv")
    for i in range(4):
        P.dma("pool", wosb[i][:, :], W["wo_d"][i * 128:(i + 1) * 128, :],
              writes=[tok(f"wo{i}")], slot="woS")

    for i in range(4):
        def z0(e, i=i):
            return e.memset(hbf0[i][:, 0:1], 0.0)
        P.op("dve", z0, writes=[tok(f"hb{i}")])

        def cp0(e, i=i):
            return e.tensor_copy(hbf0[i][:, 1:512], xbf[4 + i][:, 0:511])
        P.op("dve", cp0, reads=["xbf"], writes=[tok(f"hb{i}")])

    def hrhs(dt, ch):
        if dt < 4:
            return xbf[dt][:, ch * 512:(ch + 1) * 512], ["xbf"]
        if ch == 0:
            return hbf0[dt - 4][:, :], [tok(f"hb{dt-4}")]
        return xbf[dt][:, ch * 512 - 1:(ch + 1) * 512 - 1], ["xbf"]

    def bmul(e):
        return e.tensor_tensor(sm[0:1, 0:1][:, 0:1] if False else mem["qksc"][0:1, 0:64],
                               W["qkrow"][0:1, 0:64], W["qkrow"][0:1, 64:128], ALU.mult)
    P.op("dve", bmul, reads=[W["qkrow"].name], writes=[tok("qksc")])

    def bred(e):
        return e.tensor_reduce(sm[0:1, 46:47], mem["qksc"][0:1, 0:64], axis=AX,
                               op=ALU.max, apply_absolute_value=True)
    P.op("dve", bred, reads=[tok("qksc")], writes=[tok("bmx")])

    def bneg(e):
        return e.tensor_scalar(sm[0:1, 47:48], sm[0:1, 46:47], -8.0, -4.0, ALU.mult, ALU.add)
    P.op("dve", bneg, reads=[tok("bmx")], writes=[tok("nb")])
    P.dma("sp", bass.AP(dram["bounce"].tensor, 300, [[1, 1], [1, 1]]), sm[0:1, 47:48],
          reads=[tok("nb")], writes=[tok("dbB")], slot="dbB")
    P.dma("sp", brep[:, 0:1], bass.AP(dram["bounce"].tensor, 300, [[0, 128], [1, 1]]),
          reads=[tok("dbB")], writes=[tok("brep")], slot="brep")

    tsets = [
        {"qraw": (qraw, "qraw"), "sq": (sq, "sq"), "rf": (mem["rf32"], "rf32"),
         "xp1": (mem["xp1"], "xp1"), "xp2": (mem["xp2"], "xp2"), "bA": "a", "bB": "b"},
        {"qraw": (pT[20], "pT20"), "sq": (pT[21], "pT21"), "rf": (h2c[0], "h20"),
         "xp1": (pT[18], "pT18"), "xp2": (pT[19], "pT19"), "bA": "f", "bB": "g"},
    ]

    def l2chain(p4, ch, src_cols, dst, dsttok, which, ts):
        qr, qrt = ts["qraw"]
        sqb, sqt = ts["sq"]
        rf, rft = ts["rf"]
        x1b, x1t = ts["xp1"]
        x2b, x2t = ts["xp2"]
        bA, bB = ts["bA"], ts["bB"]

        def ev(e):
            return e.copy(qr[:, 0:512], ps[bA][:, :512])
        P.op("act", ev, reads=["ps_" + bA], writes=[tok(qrt)])

        def sqq(e):
            return e.activation(sqb[:, 0:512], qr[:, 0:512], AF.Square)
        P.op("act", sqq, reads=[tok(qrt)], writes=[tok(sqt)])

        def mmn(e):
            return e.matmul(ps[bB][:, :512], C["bd64"][:, :], sqb[:, 0:512], start=True, stop=True)
        P.op("pe", mmn, reads=[tok(sqt), "c"], writes=["ps_" + bB])

        def rln(e):
            return e.activation(rf[:, 0:512], ps[bB][:, :512], AF.Ln, bias=mem["eps18"][:, 0:1])
        P.op("act", rln, reads=["ps_" + bB, "eps18"], writes=[tok(rft)])

        def rex(e):
            return e.activation(rf[:, 0:512], rf[:, 0:512], AF.Exp, scale=-0.5)
        P.op("act", rex, reads=[tok(rft)], writes=[tok(rft)])

        def rcl(e):
            return e.tensor_scalar(x2b[:, 0:512], rf[:, 0:512], 1e12, None, ALU.min)
        P.op("dve", rcl, reads=[tok(rft)], writes=[tok(x2t)])

        def qh(e):
            return e.tensor_tensor(qr[:, 0:512], qr[:, 0:512], x2b[:, 0:512], ALU.mult)
        P.op("dve", qh, reads=[tok(qrt), tok(x2t)], writes=[tok(qrt)])

        def mrot(e):
            return e.matmul(ps[bB][:, :512], C["rot"][:, :], qr[:, 0:512], start=True, stop=True)
        P.op("pe", mrot, reads=[tok(qrt), "c"], writes=["ps_" + bB])
        scol = W["qscol"] if which == "q" else W["kscol"]
        srot = W["qsrotcol"] if which == "q" else W["ksrotcol"]

        def t1(e):
            return e.tensor_tensor(x1b[:, 0:512], qr[:, 0:512], mem["tcs"][:, :], ALU.mult)
        P.op("dve", t1, reads=[tok(qrt), tok("tcs")], writes=[tok(x1t)])

        def t1b(e, scol=scol):
            return e.tensor_scalar(x1b[:, 0:512], x1b[:, 0:512], scol[:, 0:1], None, ALU.mult)
        P.op("dve", t1b, reads=[tok(x1t), scol.name], writes=[tok(x1t)])

        def t2(e):
            return e.tensor_tensor(x2b[:, 0:512], ps[bB][:, :512], mem["tsn"][:, :], ALU.mult)
        P.op("dve", t2, reads=["ps_" + bB, tok("tsn")], writes=[tok(x2t)])

        def t2b(e, srot=srot):
            return e.tensor_scalar(x2b[:, 0:512], x2b[:, 0:512], srot[:, 0:1], None, ALU.mult)
        P.op("dve", t2b, reads=[tok(x2t), srot.name], writes=[tok(x2t)])

        def tsum(e, dst=dst, ch=ch):
            return e.tensor_tensor(dst, x1b[:, 0:512], x2b[:, 0:512], ALU.add)
        P.op("dve", tsum, reads=[tok(x1t), tok(x2t)], writes=[dsttok])

    for ch in range(NT):
        P.dma("sp", mem["tcs"][:, :], W["csk_d"][:, ch * 512:(ch + 1) * 512],
              writes=[tok("tcs")], slot="tcs")
        P.dma("sp", mem["tsn"][:, :], W["snk_d"][:, ch * 512:(ch + 1) * 512],
              writes=[tok("tsn")], slot="tsn")
        for p4 in range(4):
            ts = tsets[p4 % 2]

            def mmk(e, p4=p4, ch=ch, bA=ts["bA"]):
                last = None
                for dt in range(DT):
                    rhs, rtok = hrhs(dt, ch)
                    last = e.matmul(ps[bA][:, :512], wqkvsb[dt][:, 512 + p4 * 128:512 + (p4 + 1) * 128],
                                    rhs, start=(dt == 0), stop=(dt == DT - 1))
                return last
            rds = [tok(f"wqkv{dt}") for dt in range(DT)] + ["xbf"] + [tok(f"hb{i}") for i in range(4)]
            P.op("pe", mmk, reads=rds, writes=["ps_" + ts["bA"]])
            l2chain(p4, ch, (ch * 512, (ch + 1) * 512), k2[p4][:, ch * 512:(ch + 1) * 512],
                    tok(f"k2_{p4}"), "k", ts)

    for tt in range(16):
        def msv(e, tt=tt):
            return e.memset(vaug[tt][:, 0:VW8], 0.0)
        P.op("dve", msv, writes=[tok(f"va{tt}")])

        def onec(e, tt=tt):
            ap = bass.AP(vaug[tt], 64, [[vaug[tt].shape[1], 128], [65, 8], [1, 1]])
            return e.memset(ap, 1.0)
        P.op("dve", onec, reads=[tok(f"va{tt}")], writes=[tok(f"va{tt}")])
    for dt in range(DT):
        P.dma("pool", wqkvsb[dt][:, 512:1024], W["wqkv_d"][dt * 128:(dt + 1) * 128, 1024:1536],
              writes=[tok(f"wqkv{dt}")], slot="wqkv")
    for tt in range(16):
        ch = tt // 4
        def mmv(e, tt=tt, ch=ch):
            last = None
            for dt in range(DT):
                if dt < 4:
                    rhs = xbf[dt][:, tt * 128:(tt + 1) * 128]
                elif tt == 0:
                    rhs = hbf0[dt - 4][:, 0:128]
                else:
                    rhs = xbf[dt][:, tt * 128 - 1:(tt + 1) * 128 - 1]
                last = e.matmul(ps["a"][:, :512], rhs, wqkvsb[dt][:, 512:1024],
                                start=(dt == 0), stop=(dt == DT - 1))
            return last
        rds = [tok(f"wqkv{dt}") for dt in range(DT)] + ["xbf"] + [tok(f"hb{i}") for i in range(4)]
        P.op("pe", mmv, reads=rds, writes=["ps_a"])

        def evv(e, tt=tt):
            o = bass.AP(vaug[tt], 0, [[vaug[tt].shape[1], 128], [65, 8], [1, 64]])
            i = ps["a"].ap().rearrange("p (h c) -> p h c", c=64)[:, :, :]
            return e.copy(o, i)
        P.op("act", evv, reads=["ps_a"], writes=[tok(f"va{tt}")])

    for i in range(4):
        def zpc(e, i=i):
            return e.memset(prevcol[i][:, 0:1], 0.0)
        P.op("dve", zpc, writes=[tok(f"pc{i}")])

    for w in range(NT):
        qlo = w * 512
        P.dma("sp", mem["tcs"][:, :], W["cs8q_d"][:, qlo:qlo + 512],
              writes=[tok("tcs")], slot="tcs")
        P.dma("sp", mem["tsn"][:, :], W["sn8q_d"][:, qlo:qlo + 512],
              writes=[tok("tsn")], slot="tsn")
        for p4 in range(4):
            ts = tsets[p4 % 2]

            def mmq(e, p4=p4, w=w, bA=ts["bA"]):
                last = None
                for dt in range(DT):
                    rhs, _ = hrhs(dt, w)
                    last = e.matmul(ps[bA][:, :512], wqkvsb[dt][:, p4 * 128:(p4 + 1) * 128],
                                    rhs, start=(dt == 0), stop=(dt == DT - 1))
                return last
            rds = [tok(f"wqkv{dt}") for dt in range(DT)] + ["xbf"] + [tok(f"hb{i}") for i in range(4)]
            P.op("pe", mmq, reads=rds, writes=["ps_" + ts["bA"]])
            l2chain(p4, w, (qlo, qlo + 512), q2[p4][:, :], tok(f"q2_{p4}"), "q", ts)

        for h in range(8):
            p4 = h // 2
            base = 64 * (h % 2)
            ktiles = list(range(max(0, 4 * w - 4), 4 * w + 4))
            nk = len(ktiles)
            for ki, t in enumerate(ktiles):
                pband = "c" if ki % 2 == 0 else "d"

                def mms(e, t=t, p4=p4, base=base, pband=pband):
                    return e.matmul(ps[pband][:, :512],
                                    k2[p4][base:base + 64, t * 128:(t + 1) * 128],
                                    q2[p4][base:base + 64, :], start=True, stop=True)
                P.op("pe", mms, reads=[tok(f"k2_{p4}"), tok(f"q2_{p4}")], writes=["ps_" + pband])
                pt = ptile[ki % 2]

                def evx(e, pband=pband, pt=pt):
                    return e.activation(pt[:, :], ps[pband][:, :512], AF.Exp, bias=brep[:, 0:1])
                P.op("act", evx, reads=["ps_" + pband, tok("brep")], writes=[tok(f"pt{ki%2}")])
                dj = t - 4 * w
                mk_ = C["diagmask"] if dj >= 0 else C["prevmask"]
                mj = dj if dj >= 0 else t - (4 * w - 4)

                def msk(e, mj=mj, mk_=mk_, pt=pt):
                    return e.tensor_tensor(pt[:, :], pt[:, :], mk_[:, mj * 512:(mj + 1) * 512], ALU.mult)
                P.op("dve", msk, reads=[tok(f"pt{ki%2}"), "c"], writes=[tok(f"pt{ki%2}")])

                def mpv(e, t=t, h=h, ki=ki, nk=nk):
                    return e.matmul(ps["e"][0:65, :512], vaug[t][:, h * 65:h * 65 + 65],
                                    ptile[ki % 2][:, :], start=(ki == 0), stop=(ki == nk - 1))
                P.op("pe", mpv, reads=[tok(f"va{t}"), tok(f"pt{ki%2}")], writes=["ps_e"])

            def evo(e, p4=p4, base=base):
                return e.copy(attO[p4][base:base + 64, :], ps["e"][0:64, :512])
            P.op("act", evo, reads=["ps_e"], writes=[tok(f"at{p4}")])
            sb = ssum[h % 2]

            def evs(e, sb=sb):
                return e.copy(sb[0:1, :], ps["e"][64:65, :512])
            P.op("act", evs, reads=["ps_e"], writes=[tok(f"ss{h%2}")])
            P.dma("sp", bass.AP(dram["sums"].tensor, h * 512, [[512, 1], [1, 512]]), sb[0:1, :],
                  reads=[tok(f"ss{h%2}")], writes=[tok("dS")], slot="dS")

        P.dma("sp", r16[0:8, :], bass.AP(dram["sums"].tensor, 0, [[512, 8], [1, 512]]),
              reads=[tok("dS")], writes=[tok("r16")], slot="r16")

        def lns(e):
            return e.activation(r16[0:8, :], r16[0:8, :], AF.Ln)
        P.op("act", lns, reads=[tok("r16")], writes=[tok("r16")])

        def exns(e):
            return e.activation(r16[0:8, :], r16[0:8, :], AF.Exp, scale=-1.0)
        P.op("act", exns, reads=[tok("r16")], writes=[tok("r16")])
        P.dma("sp", dram["r16d"][0:8, :], r16[0:8, :], reads=[tok("r16")], writes=[tok("dR")], slot="dR")
        for p4 in range(4):
            rp = r2[p4 % 2]
            P.dma("sp", rp[:, :], bass.AP(dram["r16d"].tensor, 0, [[512, 2], [1024, 4], [1, 512]])[:, p4, :],
                  reads=[tok("dR")], writes=[tok(f"r2{p4%2}")], slot=f"r2{p4%2}")

            def mrep(e, rp=rp):
                return e.matmul(ps["b"][:, :512], C["e2pair"][0:2, :], rp[0:2, :], start=True, stop=True)
            P.op("pe", mrep, reads=[tok(f"r2{p4%2}"), "c"], writes=["ps_b"])

            def nrm(e, p4=p4):
                return e.tensor_tensor(attO[p4][:, :], attO[p4][:, :], ps["b"][:, :512], ALU.mult)
            P.op("dve", nrm, reads=["ps_b", tok(f"at{p4}")], writes=[tok(f"at{p4}")])

        for dt in range(DT):
            P.dma(("sp", "act", "pool")[dt % 3], xc32[dt][:, :],
                  xres_in[dt * 128:(dt + 1) * 128, qlo:qlo + 512],
                  writes=[tok(f"xl{dt}")], slot=f"xl{dt}")

            def mmo(e, dt=dt):
                last = None
                for kt4 in range(4):
                    last = e.matmul(ps["a"][:, :512], wosb[kt4][:, dt * 128:(dt + 1) * 128],
                                    attO[kt4][:, :], start=(kt4 == 0), stop=(kt4 == 3))
                return last
            P.op("pe", mmo, reads=[tok(f"wo{i}") for i in range(4)] +
                 [tok(f"at{i}") for i in range(4)], writes=["ps_a"])

            def radd(e, dt=dt):
                return e.tensor_tensor(xc32[dt][:, :], ps["a"][:, :512], xc32[dt][:, :], ALU.add)
            P.op("dve", radd, reads=[tok(f"xl{dt}"), "ps_a"], writes=[tok(f"xl{dt}")])

        for i in range(4):
            def h2a(e, i=i):
                return e.tensor_copy(h2c[i][:, 0:1], prevcol[i][:, 0:1])
            P.op("dve", h2a, reads=[tok(f"pc{i}")], writes=[tok(f"h2{i}")])

            def h2b(e, i=i):
                return e.tensor_copy(h2c[i][:, 1:512], xc32[4 + i][:, 0:511])
            P.op("dve", h2b, reads=[tok(f"xl{4+i}")], writes=[tok(f"h2{i}")])
        for i in range(4):
            def spc(e, i=i):
                return e.tensor_copy(prevcol[i][:, 0:1], xc32[4 + i][:, 511:512])
            P.op("dve", spc, reads=[tok(f"xl{4+i}")], writes=[tok(f"pc{i}")])

        def h2tile(dt):
            if dt < 4:
                return xc32[dt], tok(f"xl{dt}")
            return h2c[dt - 4], tok(f"h2{dt-4}")

        for dt in range(DT):
            h2t, h2tok = h2tile(dt)

            def cbx(e, h2t=h2t):
                return e.tensor_copy(xcbf[:, :], h2t[:, :512])
            P.op("dve", cbx, reads=[h2tok], writes=[tok("xcbf")])

            def mmsx(e, dt=dt):
                return e.matmul(ps["f"][0:1, :512], C["ones128c"][:, 0:1], xcbf[:, :512],
                                start=(dt == 0), stop=(dt == DT - 1))
            P.op("pe", mmsx, reads=[tok("xcbf"), "c"], writes=["ps_f"])

            def sqx(e, h2t=h2t):
                return e.activation(sq[:, :512], h2t[:, :512], AF.Square)
            P.op("act", sqx, reads=[h2tok], writes=[tok("sq")])

            def mmsq(e, dt=dt):
                return e.matmul(ps["g"][0:1, :512], C["ones128c"][:, 0:1], sq[:, :512],
                                start=(dt == 0), stop=(dt == DT - 1))
            P.op("pe", mmsq, reads=[tok("sq"), "c"], writes=["ps_g"])
        statm, stats = mem["statm"], mem["stats"]
        statm2 = mem["rrep"]
        rrep, mrrep = mem["rrep"], mem["mrrep"]

        def st1(e):
            return e.tensor_scalar(statm[0:1, :], ps["f"][0:1, :512], 1.0 / D, None, ALU.mult)
        P.op("dve", st1, reads=["ps_f"], writes=[tok("stm")])

        def st2(e):
            return e.tensor_scalar(stats[0:1, :], ps["g"][0:1, :512], 1.0 / D, None, ALU.mult)
        P.op("dve", st2, reads=["ps_g"], writes=[tok("sts")])

        def st3(e):
            return e.tensor_tensor(statm2[0:1, :], statm[0:1, :], statm[0:1, :], ALU.mult)
        P.op("dve", st3, reads=[tok("stm")], writes=[tok("rrep")])

        def st4(e):
            return e.tensor_tensor(stats[0:1, :], stats[0:1, :], statm2[0:1, :], ALU.subtract)
        P.op("dve", st4, reads=[tok("sts"), tok("rrep")], writes=[tok("sts")])

        def st5(e):
            return e.activation(stats[0:1, :], stats[0:1, :], AF.Ln, bias=mem["eps11"][0:1, 0:1])
        P.op("act", st5, reads=[tok("sts"), "eps11"], writes=[tok("sts")])

        def st6(e):
            return e.activation(stats[0:1, :], stats[0:1, :], AF.Exp, scale=-0.5)
        P.op("act", st6, reads=[tok("sts")], writes=[tok("sts")])

        def st7(e):
            return e.tensor_tensor(statm[0:1, :], statm[0:1, :], stats[0:1, :], ALU.mult)
        P.op("dve", st7, reads=[tok("stm"), tok("sts")], writes=[tok("stm")])

        def mr1(e):
            return e.matmul(ps["f"][:, :512], C["ones1r"][0:1, :], stats[0:1, :], start=True, stop=True)
        P.op("pe", mr1, reads=[tok("sts"), "c"], writes=["ps_f"])

        def mr2(e):
            return e.matmul(ps["g"][:, :512], C["ones1r"][0:1, :], statm[0:1, :], start=True, stop=True)
        P.op("pe", mr2, reads=[tok("stm"), "c"], writes=["ps_g"])

        def evr(e):
            return e.copy(rrep[:, :], ps["f"][:, :512])
        P.op("act", evr, reads=["ps_f"], writes=[tok("rrep")])

        def evm(e):
            return e.copy(mrrep[:, :], ps["g"][:, :512])
        P.op("act", evm, reads=["ps_g"], writes=[tok("mrrep")])
        for dt in range(DT):
            h2t, h2tok = h2tile(dt)

            def a1(e, h2t=h2t):
                return e.tensor_tensor(mem["rf32"][:, :], h2t[:, :512], rrep[:, :], ALU.mult)
            P.op("dve", a1, reads=[h2tok, tok("rrep")], writes=[tok("rf32")])

            def a2(e):
                return e.tensor_tensor(mem["rf32"][:, :], mem["rf32"][:, :], mrrep[:, :], ALU.subtract)
            P.op("dve", a2, reads=[tok("rf32"), tok("mrrep")], writes=[tok("rf32")])

            def a3(e, dt=dt):
                return e.tensor_scalar(hln[dt][:, :], mem["rf32"][:, :],
                                       W["lgsb"][:, dt:dt + 1], W["lbsb"][:, dt:dt + 1],
                                       ALU.mult, ALU.add)
            P.op("dve", a3, reads=[tok("rf32"), W["lgsb"].name, W["lbsb"].name],
                 writes=[tok(f"hln{dt}")])

        for j in range(22):
            buf = j % 2
            for gidx in range(2):
                wsrc = bass.AP(W["w1_d"].tensor, ((j * 2 + gidx) * 128) * 1024,
                               [[1024, 128], [1, 1024]])
                P.dma("pool", w1sb[buf][:, gidx * 1024:(gidx + 1) * 1024],
                      wsrc, writes=[tok(f"w1b{buf}")], slot=f"w1b{buf}")
            def mma(e, j=j, buf=buf):
                last = None
                for dt in range(DT):
                    last = e.matmul(ps["c"][:, :512], w1sb[buf][:, dt * 256:dt * 256 + 128],
                                    hln[dt][:, :], start=(dt == 0), stop=(dt == DT - 1))
                return last
            P.op("pe", mma, reads=[tok(f"w1b{buf}")] + [tok(f"hln{d}") for d in range(DT)],
                 writes=["ps_c"])

            def mmg(e, j=j, buf=buf):
                last = None
                for dt in range(DT):
                    last = e.matmul(ps["d"][:, :512], w1sb[buf][:, dt * 256 + 128:dt * 256 + 256],
                                    hln[dt][:, :], start=(dt == 0), stop=(dt == DT - 1))
                return last
            P.op("pe", mmg, reads=[tok(f"w1b{buf}")] + [tok(f"hln{d}") for d in range(DT)],
                 writes=["ps_d"])

            def eva(e, j=j):
                return e.activation(qraw[:, :], ps["c"][:, :512], AF.Identity,
                                    bias=W["b1sb"][:, 2 * j:2 * j + 1])
            P.op("act", eva, reads=["ps_c", W["b1sb"].name], writes=[tok("qraw")])

            def evg(e, j=j):
                return e.activation(sq[:, :512], ps["d"][:, :512], AF.Gelu,
                                    bias=W["b1sb"][:, 2 * j + 1:2 * j + 2])
            P.op("act", evg, reads=["ps_d", W["b1sb"].name], writes=[tok("sq")])

            def pmul(e, j=j):
                return e.tensor_tensor(pT[j][:, :], qraw[:, :], sq[:, :512], ALU.mult)
            P.op("dve", pmul, reads=[tok("qraw"), tok("sq")], writes=[tok(f"pT{j}")])

        banks = ["a", "b", "c", "d", "e", "f", "g", "h"]
        for j in range(22):
            buf = j % 2
            P.dma("pool", w2sb[buf][:, :], W["w2_d"][j * 128:(j + 1) * 128, :],
                  writes=[tok(f"w2b{buf}")], slot=f"w2b{buf}")
            for dt in range(DT):
                def mmw2(e, j=j, buf=buf, dt=dt):
                    return e.matmul(ps[banks[dt]][:, :512], w2sb[buf][:, dt * 128:(dt + 1) * 128],
                                    pT[j][:, :], start=(j == 0), stop=(j == 21))
                P.op("pe", mmw2, reads=[tok(f"w2b{buf}"), tok(f"pT{j}")], writes=["ps_" + banks[dt]])

        for dt in range(DT):
            def fadd(e, dt=dt):
                return e.scalar_tensor_tensor(xc32[dt][:, :], ps[banks[dt]][:, :512],
                                              W["b2sb"][:, dt:dt + 1], xc32[dt][:, :],
                                              ALU.add, ALU.add)
            P.op("dve", fadd, reads=[tok(f"xl{dt}"), "ps_" + banks[dt], W["b2sb"].name],
                 writes=[tok(f"xl{dt}")])
        if opts.get("last"):
            for dt in range(DT):
                xob = xo32[dt % 2]
                P.dma(("sp", "act")[dt % 2], xob[:, :],
                      opts["xorig"][dt * 128:(dt + 1) * 128, qlo:qlo + 512],
                      writes=[tok(f"xo{dt % 2}")], slot=f"xo{dt % 2}")

                def toradd(e, dt=dt, xob=xob):
                    return e.tensor_tensor(xc32[dt][:, :], xc32[dt][:, :], xob[:, :], ALU.add)
                P.op("dve", toradd, reads=[tok(f"xl{dt}"), tok(f"xo{dt % 2}")],
                     writes=[tok(f"xl{dt}")])
            emit_postln(P, mem, C, tok, xc32, xbf, qlo, xres_out, pre)
        else:
            for dt in range(DT):
                P.dma(("sp", "act", "pool")[dt % 3],
                      xres_out[dt * 128:(dt + 1) * 128, qlo:qlo + 512], xc32[dt][:, :],
                      reads=[tok(f"xl{dt}")], writes=[pre + f"dxo{dt}"], slot=f"xdo{dt}")


def alloc_ffn_buffers(alloc, sh):
    mem = {}
    mem["xbf"] = sh["xbf"]
    mem["hbf0"] = [alloc(f"hbf0_{i}", [128, 512], BF16) for i in range(4)]
    mem["wqkvsb"] = [alloc(f"wqkv{i}", [128, 1024], BF16) for i in range(DT)]
    mem["k2"] = [alloc(f"k2_{i}", [128, S], BF16) for i in range(4)]
    mem["q2"] = [alloc(f"q2_{i}", [128, 512], BF16) for i in range(4)]
    mem["qraw"] = alloc("qraw", [128, 512], BF16)
    mem["rf32"] = alloc("rf32", [128, 512], F32)
    mem["xp1"] = alloc("xp1", [128, 512], BF16)
    mem["xp2"] = alloc("xp2", [128, 512], BF16)
    mem["tcs"] = alloc("tcs", [128, 512], BF16)
    mem["tsn"] = alloc("tsn", [128, 512], BF16)
    mem["vaug"] = [alloc(f"vaug{i}", [128, 8 * 65], BF16) for i in range(16)]
    mem["attO"] = [alloc(f"attO{i}", [128, 512], BF16) for i in range(4)]
    mem["xc32"] = [alloc(f"xc32_{i}", [128, 512], F32) for i in range(DT)]
    mem["h2c"] = [alloc(f"h2c{i}", [128, 512], F32) for i in range(4)]
    mem["prevcol"] = [alloc(f"pvc{i}", [128, 1], F32) for i in range(4)]
    mem["hln"] = [alloc(f"hln{i}", [128, 512], BF16) for i in range(DT)]
    mem["pT"] = [alloc(f"pT{i}", [128, 512], BF16) for i in range(22)]
    mem["w1sb"] = [alloc(f"w1sb{i}", [128, 2048], BF16) for i in range(2)]
    mem["w2sb"] = [alloc(f"w2sb{i}", [128, 1024], BF16) for i in range(2)]
    mem["wosb"] = [alloc(f"wosb{i}", [128, 1024], BF16) for i in range(4)]
    mem["xo32"] = [alloc(f"xo32_{i}", [128, 512], F32) for i in range(2)]
    mem["sq"] = alloc("sq", [128, 512], BF16)
    mem["xcbf"] = alloc("xcbf", [128, 512], BF16)
    mem["smalls"] = alloc("smalls", [128, 48], F32)
    mem["r16"] = alloc("r16", [16, 512], F32)
    mem["r2"] = [alloc(f"r2_{i}", [2, 512], F32) for i in range(2)]
    mem["ssum"] = [t for t in mem["r2"]]
    mem["brep"] = alloc("brep", [128, 1], F32)
    mem["statm"] = alloc("statm", [1, 512], F32)
    mem["stats"] = alloc("stats", [1, 512], F32)
    mem["eps11"] = sh["eps11"]
    mem["eps18"] = sh["eps18"]
    mem["qksc"] = alloc("qksc", [1, 64], F32)
    mem["rrep"] = alloc("rrep", [128, 512], F32)
    mem["mrrep"] = alloc("mrrep", [128, 512], F32)
    mem["ptile"] = [alloc(f"ptile{i}", [128, 512], BF16) for i in range(2)]
    mem["psum"] = sh["psum"]
    return mem


def prep_ffn_host_weights(wqkv, qsc, ksc, wo, lg, lb, w1, b1, w2, b2):
    HID = 2730
    NB = 22
    d = {}
    d["wqkv"] = f32(wqkv)
    d["wo"] = f32(wo)
    w1p = np.zeros((1024, NB * 256), np.float32)
    b1p = np.zeros((128, 2 * NB), np.float32)
    for j in range(NB):
        n = min(128, HID - j * 128)
        w1p[:, j * 256:j * 256 + n] = w1[:, j * 128:j * 128 + n]
        w1p[:, j * 256 + 128:j * 256 + 128 + n] = w1[:, HID + j * 128:HID + j * 128 + n]
        b1p[:n, 2 * j] = b1[j * 128:j * 128 + n]
        b1p[:n, 2 * j + 1] = b1[HID + j * 128:HID + j * 128 + n]
    d["w1p"] = np.ascontiguousarray(
        w1p.reshape(2, 4, 128, NB, 256).transpose(3, 0, 2, 1, 4).reshape(NB * 2 * 128, 1024))
    d["b1p"] = b1p
    w2p = np.zeros((NB * 128, 1024), np.float32)
    w2p[:HID] = w2
    d["w2p"] = w2p
    d["b2"] = f32(b2.reshape(8, 128).T)
    d["lg"] = f32(lg.reshape(8, 128).T)
    d["lb"] = f32(lb.reshape(8, 128).T)
    qkrow = np.zeros((1, 128), np.float32)
    qkrow[0, :64] = qsc; qkrow[0, 64:128] = ksc
    d["qkrow"] = qkrow
    rot = lambda v: np.concatenate([v[32:], v[:32]])
    d["qscol"] = f32(np.concatenate([qsc, qsc]).reshape(128, 1))
    d["qsrotcol"] = f32(np.concatenate([rot(qsc), rot(qsc)]).reshape(128, 1))
    d["kscol"] = f32(np.concatenate([ksc, ksc]).reshape(128, 1))
    d["ksrotcol"] = f32(np.concatenate([rot(ksc), rot(ksc)]).reshape(128, 1))
    return d



def emit_quant(P, mem, tok, src, src_tok, inv, inv_tok, scl, scl_tok, out, out_tok, W512, nb):
    q = mem["q_"]

    def bview(t):
        return t[:, 0:nb].unsqueeze(2).broadcast_to([128, nb, 16])

    def rs(t):
        return t.ap().rearrange("p (b c) -> p b c", c=16)[:, 0:nb, :]

    def sg(e):
        return e.sign(q["sg"][:, :W512], src)
    P.op("act", sg, reads=[src_tok], writes=[tok("qsg")])

    def ab(e):
        return e.activation(q["ya"][:, :W512], src, AF.Abs)
    P.op("act", ab, reads=[src_tok], writes=[tok("qya")])

    def ymul(e):
        return e.tensor_tensor(rs(q["ya"]), rs(q["ya"]), bview(inv), ALU.mult)
    P.op("dve", ymul, reads=[tok("qya"), inv_tok], writes=[tok("qya")])

    y = q["ya"]

    def c0(e):
        return e.tensor_scalar(q["fB"][:, :W512], y[:, :W512], 0.25, None, ALU.is_gt)
    P.op("dve", c0, reads=[tok("qya")], writes=[tok("qfB")])
    for thr in (0.75, 1.25, 1.75):
        def ci(e, thr=thr):
            return e.scalar_tensor_tensor(q["fB"][:, :W512], y[:, :W512], thr,
                                          q["fB"][:, :W512], ALU.is_gt, ALU.add)
        P.op("dve", ci, reads=[tok("qya"), tok("qfB")], writes=[tok("qfB")])

    def half(e):
        return e.tensor_scalar(q["fB"][:, :W512], q["fB"][:, :W512], 0.5, None, ALU.mult)
    P.op("dve", half, reads=[tok("qfB")], writes=[tok("qfB")])
    for thr in (2.5, 3.5, 5.0, 5.0):
        def di(e, thr=thr):
            return e.scalar_tensor_tensor(q["fB"][:, :W512], y[:, :W512], thr,
                                          q["fB"][:, :W512], ALU.is_gt, ALU.add)
        P.op("dve", di, reads=[tok("qya"), tok("qfB")], writes=[tok("qfB")])

    def h1(e):
        return e.tensor_tensor(rs(q["fB"]), rs(q["fB"]), bview(scl), ALU.mult)
    P.op("dve", h1, reads=[tok("qfB"), scl_tok], writes=[tok("qfB")])

    def h2(e):
        return e.tensor_tensor(out, q["fB"][:, :W512], q["sg"][:, :W512], ALU.mult)
    P.op("dve", h2, reads=[tok("qfB"), tok("qsg")], writes=[out_tok])


def emit_l1ffn(P, mem, W, opts):
    pre = opts["pre"]
    dram = opts["dram"]
    C = W["consts"]
    ps = mem["psum"]
    psh = mem["psumh"]
    xc32 = mem["xc32"]

    def tok(n):
        return pre + n

    NB22 = 22

    def qslice(it, wsrcs, spills, buf):
        wn = mem["wnat"][buf]
        c0 = 0
        widths = []
        for qi, (dram_ap, wdt) in enumerate(wsrcs):
            P.dma(("sp", "act")[qi % 2], wn[:, c0:c0 + wdt],
                  dram_ap[it * 128:(it + 1) * 128, :],
                  writes=[tok(f"wn{buf}")], slot=f"wn{buf}")
            widths.append(wdt)
            c0 += wdt
        Wt = c0
        nb = Wt // 16

        def bmx(e, wn=wn, nb=nb):
            return e.tensor_reduce(mem["bm"][:, 0:nb],
                                   wn.ap().rearrange("p (b c) -> p b c", c=16)[:, 0:nb, :],
                                   axis=AX, op=ALU.max, apply_absolute_value=True)
        P.op("dve", bmx, reads=[tok(f"wn{buf}")], writes=[tok("bm")])

        def mkscl(e, nb=nb):
            return e.tensor_scalar(mem["scl"][:, 0:nb], mem["bm"][:, 0:nb], 1.0 / 6.0, 1e-12,
                                   ALU.mult, ALU.max)
        P.op("dve", mkscl, reads=[tok("bm")], writes=[tok("scl")])

        def mkinv(e, nb=nb):
            return e.reciprocal(mem["inv"][:, 0:nb], mem["scl"][:, 0:nb])
        P.op("dve", mkinv, reads=[tok("scl")], writes=[tok("inv")])
        emit_quant(P, mem, tok,
                   wn[:, 0:Wt], tok(f"wn{buf}"),
                   mem["inv"][:, :], tok("inv"),
                   mem["scl"][:, :], tok("scl"),
                   mem["wqn"][:, 0:Wt], tok("wqn"), Wt, nb)
        nblk = Wt // 128
        wq0 = (it % 2) * 512

        def trp(e, nblk=nblk):
            last = None
            for b in range(nblk):
                last = e.transpose(psh[:, b * 128:(b + 1) * 128],
                                   mem["wqn"][:, b * 128:(b + 1) * 128],
                                   C["ident"][:, :])
            return last
        P.op("pe", trp, reads=[tok("wqn"), "c"], writes=["ps_h"])

        def evt(e, nblk=nblk, wq0=wq0):
            return e.copy(mem["wqt"][:, wq0:wq0 + nblk * 128], psh[:, 0:nblk * 128])
        P.op("act", evt, reads=["ps_h"], writes=[tok(f"wqt{it % 2}")])
        b = 0
        for qi, (spill_t, sw) in enumerate(spills):
            nb2 = widths[qi] // 128
            dst = bass.AP(spill_t, it * 128, [[sw, 128], [128 * sw, nb2], [1, 128]])
            src = mem["wqt"][:, wq0 + b * 128:wq0 + (b + nb2) * 128].rearrange(
                "p (g c) -> p g c", c=128)
            P.dma("sp", dst, src, reads=[tok(f"wqt{it % 2}")],
                  writes=[tok("sp" + "gud"[qi] if len(spills) > 1 else "spd")], slot="spw")
            b += nb2

    for it in range(NB22):
        qslice(it, [(W["wgs_d"], 128), (W["wus_d"], 128)],
               [(dram["in_g"].tensor, 2816), (dram["in_u"].tensor, 2816)], it % 2)
    for it in range(8):
        qslice(it, [(W["wds_d"], 384)], [(dram["in_d"].tensor, 1024)], it % 2)

    rg = [[0, 1, 2, 3, 4, 5, 6, 7]]
    for nm, src_d, dst_d, rtok in [("g", "in_g", "qwg", "spg"), ("u", "in_u", "qwu", "spu"),
                                   ("d", "in_d", "qwd", "spd")]:
        def agx(e, src_d=src_d, dst_d=dst_d):
            return e.collective_compute("AllGather", ALU.bypass,
                                        ins=[dram[src_d]], outs=[dram[dst_d]],
                                        replica_groups=rg)
        P.op("pool", agx, reads=[tok(rtok)], writes=[tok("ag" + nm)])


    banks = ["a", "b", "c", "d", "e", "f", "g", "h"]
    for chk in range(NT):
        qlo = chk * 512
        for dt in range(DT):
            P.dma(("sp", "act", "pool")[dt % 3], xc32[dt][:, :],
                  opts["xres_in"][dt * 128:(dt + 1) * 128, qlo:qlo + 512],
                  writes=[tok(f"xl{dt}")], slot=f"xl{dt}")
        for tt in range(4):
            for half in range(2):
                def trx(e, tt=tt, half=half):
                    last = None
                    for d4 in range(4):
                        dt = half * 4 + d4
                        last = e.transpose(ps["a"][:, d4 * 128:(d4 + 1) * 128],
                                           xc32[dt][:, tt * 128:(tt + 1) * 128],
                                           C["identf"][:, :])
                    return last
                P.op("pe", trx, reads=[tok(f"xl{half*4+i}") for i in range(4)] + ["c"],
                     writes=["ps_a"])

                def evx(e, tt=tt, half=half):
                    return e.copy(mem["xnat"][tt % 2][:, half * 512:(half + 1) * 512], ps["a"][:, 0:512])
                P.op("act", evx, reads=["ps_a"], writes=[tok(f"xn{tt % 2}")])

            def bmx2(e, tt=tt):
                return e.tensor_reduce(mem["bm"][:, 0:64],
                                       mem["xnat"][tt % 2].ap().rearrange("p (b c) -> p b c", c=16),
                                       axis=AX, op=ALU.max, apply_absolute_value=True)
            P.op("dve", bmx2, reads=[tok(f"xn{tt % 2}")], writes=[tok("bm")])

            def mkscl2(e):
                return e.tensor_scalar(mem["scl"][:, 0:64], mem["bm"][:, 0:64], 1.0 / 6.0, 1e-12,
                                       ALU.mult, ALU.max)
            P.op("dve", mkscl2, reads=[tok("bm")], writes=[tok("scl")])

            def mkinv2(e):
                return e.reciprocal(mem["inv"][:, 0:64], mem["scl"][:, 0:64])
            P.op("dve", mkinv2, reads=[tok("scl")], writes=[tok("inv")])
            emit_quant(P, mem, tok,
                       mem["xnat"][tt % 2][:, 0:1024], tok(f"xn{tt % 2}"),
                       mem["inv"][:, :], tok("inv"),
                       mem["scl"][:, :], tok("scl"),
                       mem["xqn"][:, 0:1024], tok("xqn"), 1024, 64)

            def trq(e):
                last = None
                for b in range(8):
                    last = e.transpose(psh[:, b * 128:(b + 1) * 128],
                                       mem["xqn"][:, b * 128:(b + 1) * 128], C["ident"][:, :])
                return last
            P.op("pe", trq, reads=[tok("xqn"), "c"], writes=["ps_h"])

            def evq(e, tt=tt):
                o = bass.AP(mem["xqT"], tt * 128, [[4096, 128], [512, 8], [1, 128]])
                return e.copy(o, psh.ap().rearrange("p (g c) -> p g c", c=128))
            P.op("act", evq, reads=["ps_h"], writes=[tok("xqT")])

        def emit_trh(ncc):
            nw = min(512, 2816 - ncc * 512)
            nblk = nw // 128
            for tt in range(4):
                def trh(e, tt=tt, nblk=nblk, nw=nw):
                    last = None
                    for b in range(nblk):
                        last = e.transpose(psh[:, b * 128:(b + 1) * 128],
                                           mem["hqn"][:, tt * nw + b * 128:tt * nw + (b + 1) * 128],
                                           C["ident"][:, :])
                    return last
                P.op("pe", trh, reads=[tok("hqn"), "c"], writes=["ps_h"])

                def evh(e, tt=tt, nblk=nblk, ncc=ncc):
                    o = bass.AP(mem["hqT"], (ncc * 4) * 512 + tt * 128,
                                [[22 * 512, 128], [512, nblk], [1, 128]])
                    return e.copy(o, psh.ap().rearrange("p (g c) -> p g c", c=128)[:, 0:nblk, :])
                P.op("act", evh, reads=["ps_h"], writes=[tok("hqT")])

        for ncc in range(6):
            n0 = ncc * 512
            nw = min(512, 2816 - n0)
            wset = ncc % 2
            for gi, nmW in enumerate(["qwg", "qwu"]):
                for dt in range(DT):
                    wbuf = mem["wstr"][wset][gi][dt]
                    wsrc = bass.AP(dram[nmW].tensor, (dt * 128) * 2816 + n0,
                                   [[2816, 128], [1, nw]])
                    P.dma("pool", wbuf[:, 0:nw], wsrc,
                          reads=[tok("ag" + "gu"[gi])],
                          writes=[tok(f"ws{wset}_{gi}_{dt}")], slot=f"ws{gi}_{dt}")
            for tt in range(4):
                for gi, bank in [(0, "c"), (1, "d")]:
                    def mmg(e, tt=tt, gi=gi, bank=bank, nw=nw, wset=wset):
                        last = None
                        for dt in range(DT):
                            last = e.matmul(ps[bank][:, 0:nw],
                                            mem["xqT"][:, dt * 512 + tt * 128:dt * 512 + (tt + 1) * 128],
                                            mem["wstr"][wset][gi][dt][:, 0:nw],
                                            start=(dt == 0), stop=(dt == DT - 1))
                        return last
                    P.op("pe", mmg,
                         reads=[tok("xqT")] + [tok(f"ws{wset}_{gi}_{d}") for d in range(DT)],
                         writes=["ps_" + bank])

                def slu(e, nw=nw):
                    return e.activation(mem["su"][:, 0:nw], ps["c"][:, 0:nw], AF.Silu)
                P.op("act", slu, reads=["ps_c"], writes=[tok("su")])

                def hmul(e, nw=nw, tt=tt):
                    return e.tensor_tensor(mem["hm4"][:, tt * nw:(tt + 1) * nw],
                                           mem["su"][:, 0:nw], ps["d"][:, 0:nw], ALU.mult)
                P.op("dve", hmul, reads=[tok("su"), "ps_d"], writes=[tok("hm4")])

            if ncc > 0:
                emit_trh(ncc - 1)
            HB = 4 * nw
            nb = HB // 16

            def bmx3(e, HB=HB, nb=nb):
                return e.tensor_reduce(mem["bm"][:, 0:nb],
                                       mem["hm4"].ap().rearrange("p (b c) -> p b c", c=16)[:, 0:nb, :],
                                       axis=AX, op=ALU.max, apply_absolute_value=True)
            P.op("dve", bmx3, reads=[tok("hm4")], writes=[tok("bm")])

            def mkscl3(e, nb=nb):
                return e.tensor_scalar(mem["scl"][:, 0:nb], mem["bm"][:, 0:nb], 1.0 / 6.0, 1e-12,
                                       ALU.mult, ALU.max)
            P.op("dve", mkscl3, reads=[tok("bm")], writes=[tok("scl")])

            def mkinv3(e, nb=nb):
                return e.reciprocal(mem["inv"][:, 0:nb], mem["scl"][:, 0:nb])
            P.op("dve", mkinv3, reads=[tok("scl")], writes=[tok("inv")])
            emit_quant(P, mem, tok,
                       mem["hm4"][:, 0:HB], tok("hm4"),
                       mem["inv"][:, :], tok("inv"),
                       mem["scl"][:, :], tok("scl"),
                       mem["hqn"][:, 0:HB], tok("hqn"), HB, nb)
        emit_trh(5)

        for half in range(2):
            for j in range(NB22):
                wbuf = mem["w2str"][j % 2]
                src = bass.AP(dram["qwd"].tensor, (j * 128) * 1024 + half * 512,
                              [[1024, 128], [1, 512]])
                P.dma("pool", wbuf[:, 0:512], src, reads=[tok("agd")],
                      writes=[tok(f"wd{j%2}")], slot=f"wd{j%2}")
                for d4 in range(4):
                    dt = half * 4 + d4

                    def mmd(e, j=j, d4=d4):
                        return e.matmul(ps[banks[d4]][:, 0:512],
                                        mem["w2str"][j % 2][:, d4 * 128:(d4 + 1) * 128],
                                        mem["hqT"][:, j * 512:j * 512 + 512],
                                        start=(j == 0), stop=(j == NB22 - 1))
                    P.op("pe", mmd, reads=[tok(f"wd{j%2}"), tok("hqT")], writes=["ps_" + banks[d4]])
            for d4 in range(4):
                dt = half * 4 + d4

                def fadd(e, dt=dt, d4=d4):
                    return e.tensor_tensor(xc32[dt][:, :], ps[banks[d4]][:, 0:512], xc32[dt][:, :],
                                           ALU.add)
                P.op("dve", fadd, reads=[tok(f"xl{dt}"), "ps_" + banks[d4]], writes=[tok(f"xl{dt}")])
        emit_postln(P, mem, C, tok, xc32, opts.get("xbf"), qlo, opts["xres_out"], pre)


def alloc_l1ffn_buffers(alloc, sh):
    mem = {}
    mem["xc32"] = [alloc(f"xc32_{i}", [128, 512], F32) for i in range(DT)]
    mem["wnat"] = [alloc(f"wnat{i}", [128, 512], F32) for i in range(2)]
    mem["bm"] = alloc("bm", [128, 176], F32)
    mem["scl"] = alloc("scl", [128, 176], F32)
    mem["inv"] = alloc("inv", [128, 176], F32)
    mem["wqn"] = alloc("wqn", [128, 512], BF16)
    mem["wqt"] = alloc("wqt", [128, 1024], BF16)
    mem["q_"] = {k: alloc("q_" + k, [128, 2048], F32 if k == "ya" else BF16)
                 for k in ["sg", "ya", "fB"]}
    mem["xnat"] = [alloc(f"xnat{i}", [128, 1024], F32) for i in range(2)]
    mem["xqn"] = alloc("xqn", [128, 1024], BF16)
    mem["xqT"] = alloc("xqT", [128, 8 * 512], BF16)
    mem["wstr"] = [[[alloc(f"wstr{s}_{g}_{d}", [128, 512], BF16) for d in range(DT)]
                    for g in range(2)] for s in range(2)]
    mem["su"] = alloc("su", [128, 512], F32)
    mem["hm4"] = alloc("hm4", [128, 2048], F32)
    mem["hqn"] = alloc("hqn", [128, 2048], BF16)
    mem["hqT"] = alloc("hqT", [128, 22 * 512], BF16)
    mem["w2str"] = [alloc(f"w2str{i}", [128, 1024], BF16) for i in range(2)]
    mem["sq"] = alloc("sq", [128, 512], BF16)
    mem["xcbf"] = alloc("xcbf", [128, 512], BF16)
    mem["statm"] = alloc("statm", [1, 512], F32)
    mem["stats"] = alloc("stats", [1, 512], F32)
    mem["rrep"] = alloc("rrep", [128, 512], F32)
    mem["mrrep"] = alloc("mrrep", [128, 512], F32)
    mem["eps11"] = sh["eps11"]
    mem["psum"] = sh["psum"]
    mem["psumh"] = sh["psumh"]
    return mem



FFN_SMALLS = ["b1p", "b2", "lg", "lb", "qkrow", "qscol", "qsrotcol", "kscol", "ksrotcol"]
FFN_WSHAPES = [("wqkv", [1024, 1536]), ("wo", [512, 1024]), ("w1p", [22 * 2 * 128, 1024]),
               ("b1p", [128, 44]), ("w2p", [22 * 128, 1024]), ("b2", [128, 8]),
               ("lg", [128, 8]), ("lb", [128, 8]), ("qkrow", [1, 128]),
               ("qscol", [128, 1]), ("qsrotcol", [128, 1]), ("kscol", [128, 1]),
               ("ksrotcol", [128, 1])]
CONST_LIST = ["ones128c", "ones1r", "bd64", "e2pair", "diagmask", "diagneg", "slb",
              "prevmask", "rot", "ident", "identf"]
GLOBAL_CONSTS = ["ones128c", "ones1r", "bd64", "e2pair", "diagmask"]


def build_mega_program(n_stages=4, debug_mid=False):
    nc = bass.Bass("TRN2", target_bir_lowering=False)
    consts_np = {**build_attn_consts(), **build_ffn_consts()}
    ident = np.eye(128, dtype=np.float32)
    consts_np["ident"] = to_bf16(ident)
    consts_np["identf"] = f32(ident)
    est = contextlib.ExitStack()
    nc._est = est

    xin = nc.dram_tensor("xT", [D, S], F32, kind="ExternalInput")
    xout = nc.dram_tensor("xoutT", [D, S], F32, kind="ExternalOutput")
    midkind = "ExternalOutput" if debug_mid else "Internal"
    x1 = nc.dram_tensor("x1T", [D, S], F32, kind=midkind)
    xm = nc.dram_tensor("xmT", [D, S], F32, kind="Internal")
    x2 = nc.dram_tensor("x2T", [D, S], F32, kind=midkind)
    x3 = nc.dram_tensor("x3T", [D, S], F32, kind=midkind)

    dram = {
        "bounce": nc.dram_tensor("scr_bounce", [1, 512], F32, kind="Internal").ap(),
        "sums": nc.dram_tensor("scr_sums", [1, 16 * 512], F32, kind="Internal").ap(),
        "r16d": nc.dram_tensor("scr_r16", [16, 512], F32, kind="Internal").ap(),
        "qwg": nc.dram_tensor("scr_qwg", [1024, 2816], BF16, kind="Internal").ap(),
        "qwu": nc.dram_tensor("scr_qwu", [1024, 2816], BF16, kind="Internal").ap(),
        "qwd": nc.dram_tensor("scr_qwd", [3072, 1024], BF16, kind="Internal").ap(),
        "in_g": nc.dram_tensor("scr_in_g", [128, 2816], BF16, kind="Internal").ap(),
        "in_u": nc.dram_tensor("scr_in_u", [128, 2816], BF16, kind="Internal").ap(),
        "in_d": nc.dram_tensor("scr_in_d", [384, 1024], BF16, kind="Internal").ap(),
    }
    for nm in ["cs8q", "sn8q", "csk", "snk"]:
        dram[nm] = nc.dram_tensor("c_" + nm, [128, S], BF16, kind="ExternalInput").ap()
    vhld = nc.dram_tensor("c_vhl", [2, 16 * 512], BF16, kind="ExternalInput")

    a0 = {nm: nc.dram_tensor("a0_" + nm, [D, D], F32, kind="ExternalInput")
          for nm in ["Wq", "Wk", "Wv", "Wo"]}
    a0b = {nm: nc.dram_tensor("a0_" + nm, [128, 8], F32, kind="ExternalInput")
           for nm in ["bq", "bo"]}
    fd = {}
    for L in range(2):
        for nm, shp in FFN_WSHAPES:
            fd[(L, nm)] = nc.dram_tensor(f"f{L}_{nm}", shp, F32, kind="ExternalInput")
    a1 = {nm: nc.dram_tensor("a1_" + nm, [D, D], F32, kind="ExternalInput")
          for nm in ["WqT", "WkT", "WvT", "WoT"]}
    a1b = {nm: nc.dram_tensor("a1_" + nm, [128, 8], F32, kind="ExternalInput")
           for nm in ["bq", "bk", "bv", "bo"]}
    qd = {"wgs": nc.dram_tensor("q_wgs", [2816, 128], F32, kind="ExternalInput"),
          "wus": nc.dram_tensor("q_wus", [2816, 128], F32, kind="ExternalInput"),
          "wds": nc.dram_tensor("q_wds", [1024, 384], F32, kind="ExternalInput")}

    P = Prog(nc)
    sh = make_shared(nc, est)

    def gal(n, s, d):
        return est.enter_context(nc.sbuf_tensor(n, s, d))
    CG = load_consts(P, nc, gal, consts_np, GLOBAL_CONSTS)

    def mseps(e):
        return e.memset(sh["eps11"][0:1, 0:1], 1e-5)
    P.op("dve", mseps, writes=["eps11"])

    def mseps2(e):
        return e.memset(sh["eps18"][:, 0:1], 1e-18)
    P.op("dve", mseps2, writes=["eps18"])

    for dt in range(DT):
        P.dma("pool", sh["xbf"][dt][:, :], xin.ap()[dt * 128:(dt + 1) * 128, :],
              writes=["xbf"], slot="xbf")

    ar = nc.auto_arena()
    al = lambda n, s, d: ar.sbuf_tensor("s1_" + n, s, d)
    mem = alloc_attn_buffers(al, sh)
    C = {**CG, **load_consts(P, nc, al, consts_np, ["diagneg", "slb"])}
    W = {"consts": C, "slb": C["slb"], "vhl_d": vhld.ap()}
    W["setA"] = [al(f"wsetA{i}", [128, D], BF16) for i in range(DT)]
    W["setB"] = [al(f"wsetB{i}", [128, D], BF16) for i in range(DT)]
    W["wq_d"] = a0["Wq"].ap(); W["wk_d"] = a0["Wk"].ap()
    W["wv_d"] = a0["Wv"].ap(); W["wo_d"] = a0["Wo"].ap()
    for nm in ["bq", "bo"]:
        t = al("b_" + nm, [128, 8], F32)
        P.dma("sp", t[:, :], a0b[nm].ap(), writes=[t.name], slot="lw")
        W[nm] = t
    ar.allocate()
    emit_big_attn(P, mem, W, {"pre": "l0a_", "alibi": True, "xres_in": xin.ap(),
                              "xres_out": x1.ap(), "dram": dram})
    P.barrier()
    ar.free()

    if n_stages >= 2:
        ar = nc.auto_arena()
        al = lambda n, s, d: ar.sbuf_tensor("s2_" + n, s, d)
        mem = alloc_ffn_buffers(al, sh)
        C = {**CG, **load_consts(P, nc, al, consts_np, ["prevmask", "rot"])}
        Wsb = {}
        for L in range(2):
            for nm in FFN_SMALLS:
                Wsb[(L, nm)] = al(f"L{L}_{nm}", list(fd[(L, nm)].shape), F32)
        ar.allocate()
        for L in range(2):
            W = {"consts": C}
            W["wqkv_d"] = fd[(L, "wqkv")].ap(); W["wo_d"] = fd[(L, "wo")].ap()
            W["w1_d"] = fd[(L, "w1p")].ap(); W["w2_d"] = fd[(L, "w2p")].ap()
            W["cs8q_d"] = dram["cs8q"]; W["sn8q_d"] = dram["sn8q"]
            W["csk_d"] = dram["csk"]; W["snk_d"] = dram["snk"]
            for nm in FFN_SMALLS:
                t = Wsb[(L, nm)]
                P.dma("sp", t[:, :], fd[(L, nm)].ap(), writes=[t.name], slot="lw")
                key = {"b1p": "b1sb", "b2": "b2sb", "lg": "lgsb", "lb": "lbsb"}.get(nm, nm)
                W[key] = t
            if L == 0:
                xi, xo = x1.ap(), xm.ap()
            else:
                for dt in range(DT):
                    P.dma("pool", sh["xbf"][dt][:, :], xm.ap()[dt * 128:(dt + 1) * 128, :],
                          writes=["xbf"], slot="xbf",
                          reads=[f"f_dxo{d}" for d in range(DT)])
                xi, xo = xm.ap(), x2.ap()
            emit_ffn_layer(P, mem, W, {"pre": "f_", "xres_in": xi, "xres_out": xo,
                                       "dram": dram, "last": (L == 1), "xorig": x1.ap()})
        P.barrier()
        ar.free()

    if n_stages >= 3:
        ar = nc.auto_arena()
        al = lambda n, s, d: ar.sbuf_tensor("s3_" + n, s, d)
        mem = alloc_attn_buffers(al, sh)
        W = {"consts": CG}
        W["setA"] = [al(f"wsetA{i}", [128, D], BF16) for i in range(DT)]
        W["setB"] = [al(f"wsetB{i}", [128, D], BF16) for i in range(DT)]
        W["wq_d"] = a1["WqT"].ap(); W["wk_d"] = a1["WkT"].ap()
        W["wv_d"] = a1["WvT"].ap(); W["wo_d"] = a1["WoT"].ap()
        for nm in ["bq", "bk", "bv", "bo"]:
            t = al("b_" + nm, [128, 8], F32)
            P.dma("sp", t[:, :], a1b[nm].ap(), writes=[t.name], slot="lw")
            W[nm] = t
        ar.allocate()
        emit_big_attn(P, mem, W, {"pre": "l1a_", "alibi": False, "xres_in": x2.ap(),
                                  "xres_out": x3.ap(), "dram": dram})
        P.barrier()
        ar.free()

    if n_stages >= 4:
        ar = nc.auto_arena()
        al = lambda n, s, d: ar.sbuf_tensor("s4_" + n, s, d)
        mem = alloc_l1ffn_buffers(al, sh)
        C = {**CG, **load_consts(P, nc, al, consts_np, ["ident", "identf"])}
        ar.allocate()
        W = {"consts": C, "wgs_d": qd["wgs"].ap(), "wus_d": qd["wus"].ap(),
             "wds_d": qd["wds"].ap()}
        emit_l1ffn(P, mem, W, {"pre": "q_", "xres_in": x3.ap(), "xres_out": xout.ap(),
                               "dram": dram})
        P.barrier()
        ar.free()

    P.emit(est)
    return nc, consts_np


_PROGS = {}


def _get_prog(name, builder):
    if name not in _PROGS:
        _PROGS[name] = builder()
    return _PROGS[name]


def mega_in_maps(g, consts_np, cores):
    def cvec(b):
        return np.ascontiguousarray(np.asarray(b, np.float32).reshape(8, 128).T)

    base = {}
    for k in CONST_LIST + ["vhl", "cs8q", "sn8q", "csk", "snk"]:
        base["c_" + k] = np.asarray(consts_np[k])
    base["a0_Wq"] = f32(g["L0_attn_Wq"]); base["a0_Wk"] = f32(g["L0_attn_Wk"])
    base["a0_Wv"] = f32(g["L0_attn_Wv"]); base["a0_Wo"] = f32(g["L0_attn_Wo"])
    base["a0_bq"] = cvec(g["L0_attn_bq"]); base["a0_bo"] = cvec(g["L0_attn_bo"])
    for L in range(2):
        hw = prep_ffn_host_weights(g["L0_ffn_wqkv"][L], g["L0_ffn_q_scale"][L],
                                   g["L0_ffn_k_scale"][L], g["L0_ffn_wo"][L],
                                   g["L0_ffn_ln_g"][L], g["L0_ffn_ln_b"][L],
                                   g["L0_ffn_w1"][L], g["L0_ffn_b1"][L],
                                   g["L0_ffn_w2"][L], g["L0_ffn_b2"][L])
        for nm, _ in FFN_WSHAPES:
            base[f"f{L}_{nm}"] = hw[nm]
    base["a1_WqT"] = np.ascontiguousarray(f32(g["L1_attn_wq"]).T)
    base["a1_WkT"] = np.ascontiguousarray(f32(g["L1_attn_wk"]).T)
    base["a1_WvT"] = np.ascontiguousarray(f32(g["L1_attn_wv"]).T)
    base["a1_WoT"] = np.ascontiguousarray(f32(g["L1_attn_wo"]).T)
    base["a1_bq"] = cvec(g["L1_attn_bq"]); base["a1_bk"] = cvec(g["L1_attn_bk"])
    base["a1_bv"] = cvec(g["L1_attn_bv"]); base["a1_bo"] = cvec(g["L1_attn_bo"])
    wg = f32(g["L1_ffn_w_gate"]); wu = f32(g["L1_ffn_w_up"])
    wd = f32(g["L1_ffn_w_down"])
    wdp = np.zeros((1024, 3072), np.float32)
    wdp[:, :2816] = wd
    xT = [np.ascontiguousarray(f32(g["L0_attn_x"][c]).T) for c in cores]
    return [dict(base, xT=xT[c],
                 q_wgs=np.ascontiguousarray(wg[:, c * 128:(c + 1) * 128]),
                 q_wus=np.ascontiguousarray(wu[:, c * 128:(c + 1) * 128]),
                 q_wds=np.ascontiguousarray(wdp[:, c * 384:(c + 1) * 384]))
            for c in cores]


def kernel(**inputs):
    g = {k: np.asarray(v) for k, v in inputs.items()}
    cores = list(range(g["L0_attn_x"].shape[0]))
    nc, consts_np = _get_prog("mega", build_mega_program)
    maps = mega_in_maps(g, consts_np, cores)
    res = run_bass_kernel_spmd(nc, maps, core_ids=cores)
    out = np.stack([np.asarray(r["xoutT"]).T for r in res.results], 0)
    return out.astype(np.float32)
